# version 09
# baseline (speedup 1.0000x reference)
import math
import os
import numpy as np
import ml_dtypes
from contextlib import ExitStack

import concourse.bass as bass
import concourse.mybir as mybir
from concourse.bass_utils import run_bass_kernel_spmd

F32 = mybir.dt.float32
BF16 = mybir.dt.bfloat16
AF = mybir.ActivationFunctionType
ALU = mybir.AluOpType
AX = mybir.AxisListType

NCORES = 8
NB = 2
S = 2048
D = 2048
NT = S // 128
NH = 16
HD = 128
DI = 8192
NIC = DI // 128
NHC = D // 128
SCALE = 1.0 / math.sqrt(HD)
EXPC = 10.0
LNEPS = 1e-5
INVD = 1.0 / D

ENGINES = ["tensor", "vector", "scalar", "gpsimd", "sync"]
KPHASES = int(os.environ.get('KPHASES', '99'))
KF_NOMASK = os.environ.get('KF_NOMASK') == '1'
KF_NOPV = os.environ.get('KF_NOPV') == '1'
KF_NOZ = os.environ.get('KF_NOZ') == '1'
KF_NOEXP = os.environ.get('KF_NOEXP') == '1'
KF_NOST = os.environ.get('KF_NOST') == '1'
KF_QUADS = int(os.environ.get('KF_QUADS', '4'))
KF_HEADS = int(os.environ.get('KF_HEADS', str(NH)))


class _StopBuild(Exception):
    pass


class Prog:

    def __init__(self, nc):
        self.nc = nc
        self.streams = {e: [] for e in ENGINES}
        self.sems = {}
        self.ctx = ExitStack()
        self.bufuse = {}
        self.dmastreams = {}
        self.watermark = {}

    def sem(self, name):
        if name not in self.sems:
            h = self.ctx.enter_context(self.nc.semaphore(name))
            self.sems[name] = [h, 0]
        return self.sems[name]

    def dma(self, engine, fn, stream):
        name = "D_" + stream
        self.dmastreams[name] = engine
        return name, self.emit(engine, fn, inc=name, inc_by=16)

    def emit(self, engine, fn, inc=None, inc_by=1):
        rec = None
        val = None
        if inc is not None:
            s = self.sem(inc)
            s[1] += inc_by
            rec = (s[0], inc_by)
            val = s[1]
        self.streams[engine].append((fn, rec))
        return val

    def wait(self, engine, semname, value):
        if value is None or value <= 0:
            return
        wm = self.watermark.get((engine, semname), 0)
        if value <= wm:
            return
        self.watermark[(engine, semname)] = value
        s = self.sem(semname)
        assert value <= s[1], f"wait {semname}>={value} but only {s[1]} recorded"
        self.streams[engine].append(
            (lambda eng, h=s[0], v=value: eng.wait_ge(h, v), None))

    def cur(self, semname):
        return self.sem(semname)[1]

    def use(self, key, semname, value):
        d = self.bufuse.setdefault(key, {})
        if value is not None and value > d.get(semname, 0):
            d[semname] = value

    def claim(self, engine, key):
        for semname, value in self.bufuse.get(key, {}).items():
            self.wait(engine, semname, value)
        self.bufuse[key] = {}

    def waituses(self, engine, key):
        for semname, value in self.bufuse.get(key, {}).items():
            self.wait(engine, semname, value)

    def fence(self, engines=("tensor", "vector", "scalar", "gpsimd", "sync")):
        name = "fb"
        for sn, eng in list(self.dmastreams.items()):
            self.wait(eng, sn, self.cur(sn))
        for e in engines:
            if e in ("tensor", "vector", "scalar"):
                self.streams[e].append((lambda eng: eng.drain(), None))
            self.emit(e, lambda eng: eng.nop(), inc=name)
        v = self.cur(name)
        for e in engines:
            self.wait(e, name, v)
        self.bufuse = {}

    def finalize(self):
        with self.nc.Block() as block:
            for e in ENGINES:
                lst = self.streams[e]
                if not lst:
                    continue

                def body(eng, lst=lst):
                    for fn, rec in lst:
                        inst = fn(eng)
                        if rec is not None:
                            inst.then_inc(rec[0], rec[1])

                getattr(block, e)(body)
        try:
            self.ctx.close()
        except AssertionError:
            pass


def build_program():
    nc = bass.Bass("TRN2", target_bir_lowering=False)

    x_in = nc.dram_tensor("x", [NB, S, D], F32, kind="ExternalInput").ap()
    wq_f = nc.dram_tensor("wq_t", [NHC, 128, NHC, 128], F32, kind="ExternalInput").ap()
    wk_f = nc.dram_tensor("wk_t", [NHC, 128, NHC, 128], F32, kind="ExternalInput").ap()
    wv_f = nc.dram_tensor("wv_t", [NHC, 128, NHC, 128], F32, kind="ExternalInput").ap()
    wo_f = nc.dram_tensor("wo_t", [NHC, 128, D], F32, kind="ExternalInput").ap()
    wg_f = nc.dram_tensor("wg_t", [NIC, 128, NHC, 128], F32, kind="ExternalInput").ap()
    wu_f = nc.dram_tensor("wu_t", [NIC, 128, NHC, 128], F32, kind="ExternalInput").ap()
    wd_f = nc.dram_tensor("wd_t", [NIC, 128, D], F32, kind="ExternalInput").ap()
    c_id = nc.dram_tensor("c_id", [128, 128], BF16, kind="ExternalInput").ap()
    c_mask = nc.dram_tensor("c_mask", [128, 128], BF16, kind="ExternalInput").ap()
    c_onesc = nc.dram_tensor("c_onesc", [128, 1], BF16, kind="ExternalInput").ap()
    c_onesr = nc.dram_tensor("c_onesr", [1, 128], BF16, kind="ExternalInput").ap()
    y_out = nc.dram_tensor("y", [NB, S, D], F32, kind="ExternalOutput").ap()

    wq_b = nc.dram_tensor("wq_b", [NHC, 128, NHC, 128], BF16, kind="Internal").ap()
    wk_b = nc.dram_tensor("wk_b", [NHC, 128, NHC, 128], BF16, kind="Internal").ap()
    wv_b = nc.dram_tensor("wv_b", [NHC, 128, NHC, 128], BF16, kind="Internal").ap()
    wo_b = nc.dram_tensor("wo_b", [NHC, 128, D], BF16, kind="Internal").ap()
    wg_b = nc.dram_tensor("wg_b", [NIC, 128, NHC, 128], BF16, kind="Internal").ap()
    wu_b = nc.dram_tensor("wu_b", [NIC, 128, NHC, 128], BF16, kind="Internal").ap()
    wd_b = nc.dram_tensor("wd_b", [NIC, 128, D], BF16, kind="Internal").ap()
    qt_s = nc.dram_tensor("qt_s", [NH, 128, S], BF16, kind="Internal").ap()
    kt_s = nc.dram_tensor("kt_s", [NH, 128, S], BF16, kind="Internal").ap()
    ot_s = nc.dram_tensor("ot_s", [4, 128, NH, 512], BF16, kind="Internal").ap()
    xp_s = nc.dram_tensor("xp_s", [NT, 128, D], F32, kind="Internal").ap()

    P = Prog(nc)
    E = P.ctx.enter_context

    g_id = E(nc.sbuf_tensor("g_id", [128, 128], BF16))
    g_mask = E(nc.sbuf_tensor("g_mask", [128, 128], BF16))
    g_onesc = E(nc.sbuf_tensor("g_onesc", [128, 1], BF16))
    g_onesr = E(nc.sbuf_tensor("g_onesr", [1, 128], BF16))
    g_onesrf = E(nc.sbuf_tensor("g_onesrf", [1, 128], F32))
    g_cpool = E(nc.sbuf_tensor("g_cpool", [128, 2], F32))
    NSTAT = 4
    g_sum = E(nc.sbuf_tensor("g_sum", [128, NSTAT], F32))
    g_ssq = E(nc.sbuf_tensor("g_ssq", [128, NSTAT], F32))
    g_mean = E(nc.sbuf_tensor("g_mean", [128, NSTAT], F32))
    g_var = E(nc.sbuf_tensor("g_var", [128, NSTAT], F32))
    g_std = E(nc.sbuf_tensor("g_std", [128, NSTAT], F32))
    g_rstd = E(nc.sbuf_tensor("g_rstd", [128, NSTAT], F32))

    consts = [(-EXPC, 0), (LNEPS, 1)]
    for cv, j in consts:
        nc.const_aps.register(F32, cv, g_cpool[:, j:j + 1])

    for cv, j in consts:
        P.emit("gpsimd", lambda g, a=g_cpool[:, j:j + 1], v=cv: g.memset(a, v), inc="init")
    P.emit("gpsimd", lambda g, a=g_onesrf[:]: g.memset(a, 1.0), inc="init")
    for dst, csrc in ((g_id, c_id), (g_mask, c_mask), (g_onesc, c_onesc), (g_onesr, c_onesr)):
        P.dma("gpsimd", lambda g, d=dst[:], s=csrc: g.dma_start(d, s), "cst")
    P.emit("gpsimd", lambda g: g.nop(), inc="init")
    v_init = P.cur("init")
    gv_init = P.cur("D_cst")
    for e in ("tensor", "vector", "scalar", "sync"):
        P.wait(e, "init", v_init)
        P.wait(e, "D_cst", gv_init)

    wdone = {}

    def emit_reencode():
        for name, wsrc, dst, parts in (("wq", wq_f, wq_b, 4), ("wk", wk_f, wk_b, 4),
                                       ("wv", wv_f, wv_b, 4), ("wo", wo_f, wo_b, 4),
                                       ("wg", wg_f, wg_b, 16), ("wu", wu_f, wu_b, 16),
                                       ("wd", wd_f, wd_b, 16)):
            n0 = wsrc.shape[0]
            step = n0 // parts
            for i in range(parts):
                sn, _ = P.dma("gpsimd",
                              lambda g, d=dst[i * step:(i + 1) * step],
                              s=wsrc[i * step:(i + 1) * step]:
                              g.dma_start(d, s), "wr" + name)
            wdone[name] = (sn, P.cur(sn))

    def emit_ln(x_ap, out_ap, spar, dump, dumpkey, ready=()):
        for sn, v in ready:
            P.wait("scalar", sn, v)
        P.claim("scalar", dumpkey)
        P.claim("scalar", ("stat", spar))
        v_id = P.emit("scalar", lambda sc, o=dump, i=x_ap, a=g_sum[:, spar:spar + 1]:
                      sc.activation(o, i, AF.Identity, accum_out=a), inc="se")
        P.wait("scalar", "se", v_id)
        v_se = P.emit("scalar", lambda sc, o=dump, i=x_ap, a=g_ssq[:, spar:spar + 1]:
                      sc.activation(o, i, AF.Square, accum_out=a), inc="se")
        P.use(dumpkey, "se", v_se)
        P.wait("vector", "se", v_se)
        v1 = P.emit("vector", lambda v, o=g_mean[:, spar:spar + 1], i=g_sum[:, spar:spar + 1]:
                    v.tensor_scalar_mul(o, i, INVD), inc="ve")
        P.wait("vector", "ve", v1)
        v2 = P.emit("vector", lambda v, o=g_var[:, spar:spar + 1], m=g_mean[:, spar:spar + 1]:
                    v.tensor_tensor(o, m, m, ALU.mult), inc="ve")
        P.wait("vector", "ve", v2)
        v3 = P.emit("vector", lambda v, o=g_var[:, spar:spar + 1], i=g_ssq[:, spar:spar + 1],
                    m=g_var[:, spar:spar + 1]:
                    v.tensor_scalar(o, i, INVD, m, ALU.mult, ALU.subtract), inc="ve")
        P.wait("scalar", "ve", v3)
        v4 = P.emit("scalar", lambda sc, o=g_std[:, spar:spar + 1], i=g_var[:, spar:spar + 1]:
                    sc.activation(o, i, AF.Sqrt, bias=LNEPS), inc="se")
        P.wait("vector", "se", v4)
        v5 = P.emit("vector", lambda v, o=g_rstd[:, spar:spar + 1], i=g_std[:, spar:spar + 1]:
                    v.reciprocal(o, i), inc="ve")
        P.wait("vector", "ve", v5)
        v6 = P.emit("vector", lambda v, o=out_ap, i=x_ap, m=g_mean[:, spar:spar + 1],
                    r=g_rstd[:, spar:spar + 1]:
                    v.tensor_scalar(o, i, m, r, ALU.subtract, ALU.mult), inc="ve")
        P.use(("stat", spar), "ve", v6)
        return v6

    out_dma_vals = []

    NBRUN = min(NB, int(os.environ.get('KNB', str(NB))))
    try:
      for b in range(NBRUN):
        with ExitStack() as sAV:
            EV = sAV.enter_context
            V = EV(nc.sbuf_tensor(f"V{b}", [128, NT, D], BF16))

            with ExitStack() as sA:
                EA = sA.enter_context
                lnT = EA(nc.sbuf_tensor(f"lnT{b}", [128, NHC, S], BF16))
                wstat = EA(nc.sbuf_tensor(f"wstat{b}", [128, 2, NHC, 128], BF16))
                xb = EA(nc.sbuf_tensor(f"xbA{b}", [128, 4, D], F32))
                lnb = EA(nc.sbuf_tensor(f"lnbA{b}", [128, 2, D], BF16))
                eq = EA(nc.sbuf_tensor(f"eqA{b}", [128, 2, D], BF16))
                dumpA = EA(nc.sbuf_tensor(f"dumpA{b}", [128, D], F32))
                PH = [EA(nc.psum_tensor(f"PH{b}{i}", [128, 1024], F32)) for i in range(2)]
                g_pt = [EA(nc.psum_tensor(f"ptA{b}{i}", [128, 1024], BF16)) for i in range(2)]

                lnT_done = 0
                INVD2 = INVD * INVD
                ldv = {}
                v_se = {}
                v_var = {}
                v_std = {}
                lv = {}

                def a1_ld(t):
                    sl = t % 4
                    P.claim("scalar", ("xbA", sl))
                    ldv[t] = P.dma("scalar",
                                   lambda g, d=xb[:, sl, :], s=x_in[b, t * 128:(t + 1) * 128, :]:
                                   g.dma_start(d, s), f"xA{sl}")

                def a1_s1(t):
                    sl = t % 4
                    P.wait("scalar", *ldv[t])
                    P.claim("scalar", ("stat", sl))
                    P.claim("scalar", "dumpA")
                    vid = P.emit("scalar", lambda sc, o=dumpA[:], i=xb[:, sl, :],
                                 a=g_sum[:, sl:sl + 1]:
                                 sc.activation(o, i, AF.Identity, accum_out=a), inc="se")
                    P.wait("scalar", "se", vid)
                    v_se[t] = P.emit("scalar", lambda sc, o=dumpA[:], i=xb[:, sl, :],
                                     a=g_ssq[:, sl:sl + 1]:
                                     sc.activation(o, i, AF.Square, accum_out=a), inc="se")
                    P.use("dumpA", "se", v_se[t])

                def a1_s2(t):
                    sl = t % 4
                    P.wait("vector", "se", v_se[t])
                    m1 = P.emit("vector", lambda v, o=g_var[:, sl:sl + 1], i=g_sum[:, sl:sl + 1],
                                s1=g_sum[:, sl:sl + 1]:
                                v.tensor_scalar(o, i, s1, INVD2, ALU.mult, ALU.mult), inc="ve")
                    P.emit("vector", lambda v, o=g_mean[:, sl:sl + 1], i=g_sum[:, sl:sl + 1]:
                           v.tensor_scalar_mul(o, i, INVD), inc="ve")
                    P.wait("vector", "ve", m1)
                    v_var[t] = P.emit("vector", lambda v, o=g_var[:, sl:sl + 1],
                                      i=g_ssq[:, sl:sl + 1], m=g_var[:, sl:sl + 1]:
                                      v.tensor_scalar(o, i, INVD, m, ALU.mult, ALU.subtract),
                                      inc="ve")

                def a1_s3(t):
                    sl = t % 4
                    P.wait("scalar", "ve", v_var[t])
                    v_std[t] = P.emit("scalar", lambda sc, o=g_std[:, sl:sl + 1],
                                      i=g_var[:, sl:sl + 1]:
                                      sc.activation(o, i, AF.Sqrt, bias=LNEPS), inc="se")

                def a1_s4(t):
                    sl = t % 4
                    P.wait("vector", "se", v_std[t])
                    r1 = P.emit("vector", lambda v, o=g_rstd[:, sl:sl + 1], i=g_std[:, sl:sl + 1]:
                                v.reciprocal(o, i), inc="ve")
                    P.wait("vector", "ve", r1)
                    P.claim("vector", ("lnbA", t % 2))
                    lv[t] = P.emit("vector", lambda v, o=lnb[:, t % 2, :], i=xb[:, sl, :],
                                   m=g_mean[:, sl:sl + 1], r=g_rstd[:, sl:sl + 1]:
                                   v.tensor_scalar(o, i, m, r, ALU.subtract, ALU.mult), inc="ve")
                    P.use(("xbA", sl), "ve", lv[t])
                    P.use(("stat", sl), "ve", lv[t])

                def a1_tr(t):
                    nonlocal lnT_done
                    P.wait("tensor", "ve", lv[t])
                    for grp in range(2):
                        P.claim("tensor", ("pt", grp))
                        tv = None
                        for j in range(8):
                            hc = grp * 8 + j
                            tv = P.emit(
                                "tensor",
                                lambda te, o=g_pt[grp][:, j * 128:(j + 1) * 128],
                                i=lnb[:, t % 2, hc * 128:(hc + 1) * 128]:
                                te.transpose(o, i, g_id[:]),
                                inc="pe" if j == 7 else None)
                        P.use(("lnbA", t % 2), "pe", tv)
                        P.wait("vector", "pe", tv)
                        ev = P.emit(
                            "vector",
                            lambda v, o=lnT[:, grp * 8:(grp + 1) * 8, t * 128:(t + 1) * 128],
                            i=g_pt[grp].ap().rearrange("p (c f) -> p c f", c=8):
                            v.tensor_copy(o, i), inc="ve")
                        P.use(("pt", grp), "ve", ev)
                        lnT_done = ev

                for t in range(4):
                    a1_ld(t)
                for step in range(NT + 3):
                    if step < NT:
                        a1_s1(step)
                    if 0 <= step - 1 < NT:
                        a1_s2(step - 1)
                    if 0 <= step - 2 < NT:
                        a1_s3(step - 2)
                    if 0 <= step - 3 < NT:
                        a1_s4(step - 3)
                        if step + 1 < NT:
                            a1_ld(step + 1)
                        a1_tr(step - 3)

                def qkv_phase(w_bf, spill_dst, wname, is_v):
                    P.wait("sync", *wdone[wname])
                    spills = {}
                    loadv = {}

                    def load(oc):
                        par = oc % 2
                        P.claim("sync", ("wstat", par))
                        loadv[oc] = P.dma(
                            "sync", lambda s, d=wstat[:, par], src=w_bf[oc]:
                            s.dma_start(d, src), f"ws{par}")

                    load(0)
                    load(1)
                    for oc in range(NHC):
                        par = oc % 2
                        P.wait("tensor", *loadv[oc])
                        P.wait("tensor", "ve", lnT_done)
                        mmv = [0, 0]
                        for sh in range(2):
                            P.claim("tensor", ("PH", sh))
                            for hc in range(NHC):
                                for half in range(2):
                                    st0 = (hc == 0)
                                    sp0 = (hc == NHC - 1)
                                    vv = P.emit(
                                        "tensor",
                                        lambda te, o=PH[sh][:, half * 512:(half + 1) * 512],
                                        w=wstat[:, par, hc, :],
                                        r=lnT[:, hc, sh * 1024 + half * 512:sh * 1024 + (half + 1) * 512],
                                        s0=st0, s1=sp0:
                                        te.matmul(o, w, r, start=s0, stop=s1),
                                        inc="pe" if (sp0 and half == 1) else None)
                            mmv[sh] = vv
                            P.use(("wstat", par), "pe", vv)
                            P.wait("vector", "pe", vv)
                            P.claim("vector", ("eq", par, sh))
                            ev = P.emit(
                                "vector",
                                lambda v, o=eq[:, par, sh * 1024:(sh + 1) * 1024], i=PH[sh][:]:
                                v.tensor_copy(o, i), inc="ve")
                            P.use(("PH", sh), "ve", ev)
                            if is_v:
                                P.wait("tensor", "ve", ev)
                                P.claim("tensor", ("pt", sh))
                                tv = None
                                for j in range(8):
                                    tv = P.emit(
                                        "tensor",
                                        lambda te, o=g_pt[sh][:, j * 128:(j + 1) * 128],
                                        i=eq[:, par, sh * 1024 + j * 128:sh * 1024 + (j + 1) * 128]:
                                        te.transpose(o, i, g_id[:]),
                                        inc="pe" if j == 7 else None)
                                P.use(("eq", par, sh), "pe", tv)
                                P.wait("vector", "pe", tv)
                                vev = P.emit(
                                    "vector",
                                    lambda v, o=V[:, sh * 8:(sh + 1) * 8, oc * 128:(oc + 1) * 128],
                                    i=g_pt[sh].ap().rearrange("p (c f) -> p c f", c=8):
                                    v.tensor_copy(o, i), inc="ve")
                                P.use(("pt", sh), "ve", vev)
                        if not is_v:
                            P.wait("sync", "ve", P.cur("ve"))
                            ssn, sv = P.dma(
                                "sync", lambda g, d=spill_dst[oc], s=eq[:, par, :]:
                                g.dma_start(d, s), f"sp{par}")
                            P.use(("eq", par, 0), ssn, sv)
                            P.use(("eq", par, 1), ssn, sv)
                            spills[ssn] = sv
                        if oc + 2 < NHC:
                            load(oc + 2)
                    return spills

                if KPHASES <= 1:
                    raise _StopBuild
                if b == 0:
                    for sl in range(4):
                        P.wait("gpsimd", f"D_xA{sl}", P.cur(f"D_xA{sl}"))
                    emit_reencode()
                q_spill_done = qkv_phase(wq_b, qt_s, "wq", False)
                k_spill_done = qkv_phase(wk_b, kt_s, "wk", False)
                qkv_phase(wv_b, None, "wv", True)
                if KPHASES <= 2:
                    raise _StopBuild

            P.fence()

            with ExitStack() as sF:
                EF = sF.enter_context
                kT = EF(nc.sbuf_tensor(f"kT{b}", [128, NH, S], BF16))
                qTq = EF(nc.sbuf_tensor(f"qTq{b}", [128, NH, 512], BF16))
                ET = [EF(nc.sbuf_tensor(f"ET{b}{i}", [128, NT, 512], BF16)) for i in range(2)]
                otq = EF(nc.sbuf_tensor(f"otq{b}", [128, NH, 512], BF16))
                crow = EF(nc.sbuf_tensor(f"crow{b}", [1, 2, 512], BF16))
                bcs = EF(nc.sbuf_tensor(f"bcs{b}", [128, 512], F32))
                ST = [EF(nc.psum_tensor(f"ST{b}{i}", [128, 1024], F32)) for i in range(2)]
                OTp = [EF(nc.psum_tensor(f"OTp{b}{i}", [128, 512], F32)) for i in range(2)]
                Zp = EF(nc.psum_tensor(f"Zp{b}", [1, 512], F32))
                BCp = EF(nc.psum_tensor(f"BCp{b}", [128, 512], F32))

                ot_spill = {}
                for p in range(KF_QUADS):
                    qc0 = p * 512
                    for sn, vv in list(q_spill_done.items()) + list(k_spill_done.items()):
                        P.wait("sync", sn, vv)
                    P.claim("sync", "qTq")
                    for hh in range(NH):
                        qsn, qt_load = P.dma(
                            "sync", lambda s, d=qTq[:, hh, :], src=qt_s[hh, :, qc0:qc0 + 512]:
                            s.dma_start(d, src), "fq")
                    for hh in range(NH):
                        ksn, kt_load = P.dma(
                            "sync", lambda s, d=kT[:, hh, qc0:qc0 + 512],
                            src=kt_s[hh, :, qc0:qc0 + 512]:
                            s.dma_start(d, src), "fk")

                    kmax = 4 * p + 3
                    for h in range(KF_HEADS):
                        etb = ET[h % 2]
                        P.wait("tensor", "D_fq", qt_load)
                        P.wait("tensor", "D_fk", kt_load)
                        stv = {}
                        expv = {}
                        P.claim("scalar", ("ET", h % 2))
                        for kc0 in range(0, kmax + 1, 2):
                            stp = (kc0 // 2) % 2
                            P.claim("tensor", ("ST", stp))
                            offs = []
                            for kc in (kc0, kc0 + 1):
                                if kc > kmax:
                                    continue
                                off = max(kc - 4 * p, 0) * 128
                                offs.append((kc, off))
                                stv[kc] = P.emit(
                                    "tensor",
                                    lambda te, o=ST[stp][:, (kc % 2) * 512 + off:(kc % 2) * 512 + 512],
                                    w=kT[:, h, kc * 128:(kc + 1) * 128], r=qTq[:, h, off:512]:
                                    te.matmul(o, w, r, start=True, stop=True), inc="pe")
                                P.use("qTq", "pe", stv[kc])
                            last_mm = stv[offs[-1][0]]
                            P.wait("scalar", "pe", last_mm)
                            if len(offs) == 2 and offs[0][1] == 0 and offs[1][1] == 0:
                                ev = P.emit(
                                    "scalar",
                                    lambda sc, o=etb[:, kc0:kc0 + 2, :], i=ST[stp][:]:
                                    sc.activation(o, i, AF.Exp, bias=-EXPC, scale=SCALE),
                                    inc="se")
                                expv[kc0] = expv[kc0 + 1] = ev
                            else:
                                for kc, off in offs:
                                    expv[kc] = P.emit(
                                        "scalar",
                                        lambda sc, o=etb[:, kc, off:512],
                                        i=ST[stp][:, (kc % 2) * 512 + off:(kc % 2) * 512 + 512]:
                                        sc.activation(o, i, AF.Exp, bias=-EXPC, scale=SCALE),
                                        inc="se")
                            P.use(("ST", stp), "se", expv[offs[-1][0]])
                        mv = None
                        for j in range(4 if not KF_NOMASK else 0):
                            kc = 4 * p + j
                            off = j * 128
                            P.wait("vector", "se", expv[kc])
                            mv = P.emit(
                                "vector",
                                lambda v, a=etb[:, kc, off:off + 128]:
                                v.tensor_tensor(a, a, g_mask[:], ALU.mult), inc="ve")
                        if KF_NOPV:
                            continue
                        P.wait("tensor", "se", expv[kmax])
                        P.wait("tensor", "ve", mv)
                        otp = OTp[h % 2]
                        P.claim("tensor", ("OTp", h % 2))
                        P.claim("tensor", "Zp")
                        pvz = None
                        for kc in range(kmax + 1):
                            off = max(kc - 4 * p, 0) * 128
                            P.emit(
                                "tensor",
                                lambda te, o=otp[:, off:512], w=V[:, kc, h * 128:(h + 1) * 128],
                                r=etb[:, kc, off:512], s0=(kc == 0), s1=(kc == kmax):
                                te.matmul(o, w, r, start=s0, stop=s1))
                        for kc in range(kmax + 1):
                            off = max(kc - 4 * p, 0) * 128
                            pvz = P.emit(
                                "tensor",
                                lambda te, o=Zp[0:1, off:512], w=g_onesc[:],
                                r=etb[:, kc, off:512], s0=(kc == 0), s1=(kc == kmax):
                                te.matmul(o, w, r, start=s0, stop=s1), inc="pe")
                        P.use(("ET", h % 2), "pe", pvz)
                        if KF_NOZ:
                            P.wait("vector", "pe", pvz)
                            P.claim("vector", ("otq", h))
                            oev = P.emit("vector", lambda v, o=otq[:, h, :], i=otp[:]:
                                         v.tensor_copy(o, i), inc="ve")
                            P.use(("OTp", h % 2), "ve", oev)
                        else:
                            P.wait("vector", "pe", pvz)
                            P.claim("vector", ("crow", h % 2))
                            def _recipb(v, o=crow[0:1, h % 2, :], i=Zp[0:1, :], _nc=nc):
                                with _nc.allow_low_precision("crow bf16: per-row scale"):
                                    return v.reciprocal(o, i)
                            rv = P.emit("vector", _recipb, inc="ve")
                            P.use("Zp", "ve", rv)
                            P.wait("tensor", "ve", rv)
                            P.claim("tensor", "BCp")
                            bcv = P.emit("tensor",
                                         lambda te, o=BCp[:], w=g_onesr[:], r=crow[0:1, h % 2, :]:
                                         te.matmul(o, w, r, start=True, stop=True), inc="pe")
                            P.use(("crow", h % 2), "pe", bcv)
                            P.wait("vector", "pe", bcv)
                            P.claim("vector", "bcs")
                            cv = P.emit("vector", lambda v, o=bcs[:], i=BCp[:]:
                                        v.tensor_copy(o, i), inc="ve")
                            P.use("BCp", "ve", cv)
                            P.claim("vector", ("otq", h))
                            P.wait("vector", "ve", cv)
                            oev = P.emit("vector", lambda v, o=otq[:, h, :], i=otp[:], c=bcs[:]:
                                         v.tensor_tensor(o, i, c, ALU.mult), inc="ve")
                            P.use(("OTp", h % 2), "ve", oev)
                            P.use("bcs", "ve", oev)
                    P.wait("gpsimd", "ve", P.cur("ve"))
                    osn, sv = P.dma("gpsimd", lambda g, d=ot_s[p], s=otq[:]:
                                    g.dma_start(d, s), "os")
                    for h in range(NH):
                        P.use(("otq", h), osn, sv)
                    ot_spill[p] = (osn, sv)

            P.fence()
            if KPHASES <= 3:
                raise _StopBuild

        with ExitStack() as sW:
            EW = sW.enter_context
            wo_res = EW(nc.sbuf_tensor(f"wo_res{b}", [128, NHC, D], BF16))
            otq_wo = EW(nc.sbuf_tensor(f"otq_wo{b}", [128, 2, NH, 512], BF16))
            xbw = EW(nc.sbuf_tensor(f"xbW{b}", [128, 2, D], F32))
            attn_t = EW(nc.sbuf_tensor(f"attn_t{b}", [128, D], F32))
            lnw = EW(nc.sbuf_tensor(f"lnw{b}", [128, D], F32))
            xpo = EW(nc.sbuf_tensor(f"xpoW{b}", [128, 2, D], F32))
            dumpW = EW(nc.sbuf_tensor(f"dumpW{b}", [128, D], F32))
            PW = [EW(nc.psum_tensor(f"PW{b}{i}", [128, 1024], F32)) for i in range(2)]

            P.wait("sync", *wdone["wo"])
            for fc in range(NHC):
                wsn, wo_load = P.dma("sync", lambda s, d=wo_res[:, fc, :], src=wo_b[fc]:
                                     s.dma_start(d, src), "wo")
            spillw = [0] * NT
            for t in range(NT):
                p = t // 4
                parq = p % 2
                if t % 4 == 0:
                    P.wait("sync", *ot_spill[p])
                    P.claim("sync", ("otq_wo", parq))
                    oqsn, oq_load = P.dma(
                        "sync", lambda s, d=otq_wo[:, parq], src=ot_s[p]:
                        s.dma_start(d, src), f"oq{parq}")
                P.claim("scalar", ("xbW", t % 2))
                xwsn, xv = P.dma(
                    "scalar", lambda g, d=xbw[:, t % 2, :], s=x_in[b, t * 128:(t + 1) * 128, :]:
                    g.dma_start(d, s), f"xW{t % 2}")
                P.wait("tensor", "D_wo", wo_load)
                P.wait("tensor", oqsn, oq_load)
                P.claim("scalar", ("stat", 0))
                P.claim("scalar", ("stat", 1))
                last_oq_use = None
                for oh in range(2):
                    P.claim("tensor", ("PW", oh))
                    vv = None
                    for fc in range(NHC):
                        for half in range(2):
                            st0 = (fc == 0)
                            sp0 = (fc == NHC - 1)
                            vv = P.emit(
                                "tensor",
                                lambda te, o=PW[oh][:, half * 512:(half + 1) * 512],
                                w=otq_wo[:, parq, fc, (t % 4) * 128:(t % 4) * 128 + 128],
                                r=wo_res[:, fc, oh * 1024 + half * 512:oh * 1024 + (half + 1) * 512],
                                s0=st0, s1=sp0:
                                te.matmul(o, w, r, start=s0, stop=s1),
                                inc="pe" if (sp0 and half == 1) else None)
                    last_oq_use = vv
                    P.wait("scalar", "pe", vv)
                    if oh == 0:
                        P.claim("scalar", ("attn_t",))
                    ev = P.emit(
                        "scalar",
                        lambda sc, o=attn_t[:, oh * 1024:(oh + 1) * 1024], i=PW[oh][:],
                        a=g_sum[:, oh:oh + 1]:
                        sc.activation(o, i, AF.Identity, accum_out=a), inc="se")
                    P.use(("PW", oh), "se", ev)
                P.use(("otq_wo", parq), "pe", last_oq_use)
                P.claim("scalar", ("dumpW",))
                P.wait("scalar", "se", P.cur("se"))
                v_sq = P.emit("scalar", lambda sc, o=dumpW[:], i=attn_t[:], a=g_ssq[:, 0:1]:
                              sc.activation(o, i, AF.Square, accum_out=a), inc="se")
                P.use(("dumpW",), "se", v_sq)
                P.wait("vector", "se", v_sq)
                w1 = P.emit("vector", lambda v, o=g_sum[:, 0:1], a=g_sum[:, 0:1], bb=g_sum[:, 1:2]:
                            v.tensor_tensor(o, a, bb, ALU.add), inc="ve")
                P.wait("vector", "ve", w1)
                w2 = P.emit("vector", lambda v, o=g_mean[:, 0:1], i=g_sum[:, 0:1]:
                            v.tensor_scalar_mul(o, i, INVD), inc="ve")
                P.wait("vector", "ve", w2)
                w3 = P.emit("vector", lambda v, o=g_var[:, 0:1], m=g_mean[:, 0:1]:
                            v.tensor_tensor(o, m, m, ALU.mult), inc="ve")
                P.wait("vector", "ve", w3)
                w4 = P.emit("vector", lambda v, o=g_var[:, 0:1], i=g_ssq[:, 0:1], m=g_var[:, 0:1]:
                            v.tensor_scalar(o, i, INVD, m, ALU.mult, ALU.subtract), inc="ve")
                P.wait("scalar", "ve", w4)
                w5 = P.emit("scalar", lambda sc, o=g_std[:, 0:1], i=g_var[:, 0:1]:
                            sc.activation(o, i, AF.Sqrt, bias=LNEPS), inc="se")
                P.wait("vector", "se", w5)
                w6 = P.emit("vector", lambda v, o=g_rstd[:, 0:1], i=g_std[:, 0:1]:
                            v.reciprocal(o, i), inc="ve")
                P.wait("vector", "ve", w6)
                P.claim("vector", ("lnw",))
                w7 = P.emit("vector", lambda v, o=lnw[:], i=attn_t[:], m=g_mean[:, 0:1],
                            r=g_rstd[:, 0:1]:
                            v.tensor_scalar(o, i, m, r, ALU.subtract, ALU.mult), inc="ve")
                P.use(("attn_t",), "ve", w7)
                P.use(("stat", 0), "ve", w7)
                P.use(("stat", 1), "ve", w7)
                P.wait("vector", xwsn, xv)
                P.wait("vector", "ve", w7)
                P.claim("vector", ("xpo", t % 2))
                w8 = P.emit("vector", lambda v, o=xpo[:, t % 2, :], i=lnw[:], xx=xbw[:, t % 2, :]:
                            v.tensor_tensor(o, i, xx, ALU.add), inc="ve")
                P.use(("lnw",), "ve", w8)
                P.use(("xbW", t % 2), "ve", w8)
                P.wait("gpsimd", "ve", w8)
                xssn, sv = P.dma("gpsimd", lambda g, d=xp_s[t], s=xpo[:, t % 2, :]:
                                 g.dma_start(d, s), f"xs{t % 2}")
                P.use(("xpo", t % 2), xssn, sv)
                spillw[t] = (xssn, sv)

        P.fence()
        if KPHASES <= 4:
            raise _StopBuild

        with ExitStack() as sN:
            EN = sN.enter_context
            lnT2 = EN(nc.sbuf_tensor(f"lnT2{b}", [128, 2, NHC, 512], BF16))
            hspT = EN(nc.sbuf_tensor(f"hspT{b}", [128, NIC, 512], BF16))
            hbuf = EN(nc.sbuf_tensor(f"hbuf{b}", [128, 4, 512], BF16))
            habs = EN(nc.sbuf_tensor(f"habs{b}", [128, 4, 512], BF16))
            htmp = EN(nc.sbuf_tensor(f"htmp{b}", [128, 4, 512], BF16))
            sg = EN(nc.sbuf_tensor(f"sg{b}", [128, 2, 512], F32))
            wgu = EN(nc.sbuf_tensor(f"wgu{b}", [128, 2, 2, NHC, 128], BF16))
            wdc = EN(nc.sbuf_tensor(f"wdc{b}", [128, 2, 8, 512], BF16))
            xbn = EN(nc.sbuf_tensor(f"xbN{b}", [128, 2, D], F32))
            lnbn = EN(nc.sbuf_tensor(f"lnbN{b}", [128, 2, D], BF16))
            ffo = EN(nc.sbuf_tensor(f"ffo{b}", [128, 4, D], BF16))
            lnn = EN(nc.sbuf_tensor(f"lnn{b}", [128, 2, D], F32))
            dumpN = EN(nc.sbuf_tensor(f"dumpN{b}", [128, D], BF16))
            PF = [EN(nc.psum_tensor(f"PF{b}{i}", [128, 512], F32)) for i in range(4)]
            g_pt = [EN(nc.psum_tensor(f"ptN{b}{i}", [128, 1024], BF16)) for i in range(2)]

            P.wait("sync", *wdone["wg"])
            P.wait("sync", *wdone["wu"])
            P.wait("sync", *wdone["wd"])

            ln2_done = {}

            def emit_F0(g):
                gp = g % 2
                P.claim("vector", ("lnT2", gp))
                for tl in range(4):
                    t = g * 4 + tl
                    par = tl % 2
                    P.wait("scalar", *spillw[t])
                    P.claim("scalar", ("xbN", par))
                    xnsn, xv = P.dma("scalar", lambda gg, d=xbn[:, par, :], s=xp_s[t]:
                                     gg.dma_start(d, s), f"xN{par}")
                    P.claim("vector", ("lnbN", par))
                    lv = emit_ln(xbn[:, par, :], lnbn[:, par, :], 2 + par, dumpN[:], "dumpN",
                                 ready=[(xnsn, xv)])
                    P.use(("xbN", par), "ve", lv)
                    P.wait("tensor", "ve", lv)
                    for grp in range(2):
                        P.claim("tensor", ("pt", grp))
                        tv = None
                        for j in range(8):
                            hc = grp * 8 + j
                            tv = P.emit(
                                "tensor",
                                lambda te, o=g_pt[grp][:, j * 128:(j + 1) * 128],
                                i=lnbn[:, par, hc * 128:(hc + 1) * 128]:
                                te.transpose(o, i, g_id[:]),
                                inc="pe" if j == 7 else None)
                        P.use(("lnbN", par), "pe", tv)
                        P.wait("vector", "pe", tv)
                        ev = P.emit(
                            "vector",
                            lambda v, o=lnT2[:, gp, grp * 8:(grp + 1) * 8, tl * 128:(tl + 1) * 128],
                            i=g_pt[grp].ap().rearrange("p (c f) -> p c f", c=8):
                            v.tensor_copy(o, i), inc="ve")
                        P.use(("pt", grp), "ve", ev)
                        ln2_done[g] = ev

            emit_F0(0)
            for g in range(4):
                gp = g % 2
                ln2_last = ln2_done[g]

                seq = [(c4, sub) for c4 in range(16) for sub in range(4)]
                guload = {}

                def gu_load(idx):
                    c4, sub = seq[idx]
                    parb = sub % 2
                    for pj, wsrc in enumerate((wg_b, wu_b)):
                        ic = c4 + 16 * sub
                        P.claim("sync", ("wgu", parb, pj))
                        guload[(c4, sub, pj)] = P.dma(
                            "sync", lambda s, d=wgu[:, parb, pj], src=wsrc[ic]:
                            s.dma_start(d, src), f"wg{parb}{pj}")

                gu_load(0)
                gu_load(1)
                for idx, (c4, sub) in enumerate(seq):
                    ic = c4 + 16 * sub
                    parb = sub % 2
                    for pj in range(2):
                        bank = 2 * parb + pj
                        P.wait("tensor", *guload[(c4, sub, pj)])
                        P.wait("tensor", "ve", ln2_last)
                        P.claim("tensor", ("PF", bank))
                        vv = None
                        for hc in range(NHC):
                            st0 = (hc == 0)
                            sp0 = (hc == NHC - 1)
                            vv = P.emit(
                                "tensor",
                                lambda te, o=PF[bank][:], w=wgu[:, parb, pj, hc, :],
                                r=lnT2[:, gp, hc, :], s0=st0, s1=sp0:
                                te.matmul(o, w, r, start=s0, stop=s1),
                                inc="pe" if sp0 else None)
                        P.use(("wgu", parb, pj), "pe", vv)
                        P.use(("lnT2", gp), "pe", vv)
                        if pj == 0:
                            gate_mm = vv
                        else:
                            up_mm = vv
                    P.wait("scalar", "pe", gate_mm)
                    P.claim("scalar", ("sg", parb))
                    sl = P.emit("scalar", lambda sc, o=sg[:, parb, :], i=PF[2 * parb][:]:
                                sc.activation(o, i, AF.Silu), inc="se")
                    P.use(("PF", 2 * parb), "se", sl)
                    P.wait("vector", "se", sl)
                    P.wait("vector", "pe", up_mm)
                    P.claim("vector", ("hbuf", sub))
                    hv = P.emit("vector", lambda v, o=hbuf[:, sub, :], i=sg[:, parb, :],
                                u=PF[2 * parb + 1][:]:
                                v.tensor_tensor(o, i, u, ALU.mult), inc="ve")
                    P.use(("sg", parb), "ve", hv)
                    P.use(("PF", 2 * parb + 1), "ve", hv)
                    if sub == 3:
                        def vop(fn, inc=None):
                            return P.emit("vector", fn, inc=inc)
                        P.wait("vector", "ve", hv)
                        for s4 in range(4):
                            vop(lambda v, o=habs[:, s4, :], i=hbuf[:, s4, :]:
                                v.tensor_tensor(o, i, i, ALU.mult),
                                inc="ve" if s4 == 3 else None)
                        P.wait("vector", "ve", P.cur("ve"))
                        vop(lambda v, o=htmp[:, 0, :], a=habs[:, 0, :], bb=habs[:, 1, :]:
                            v.tensor_tensor(o, a, bb, ALU.max))
                        vop(lambda v, o=htmp[:, 1, :], a=habs[:, 0, :], bb=habs[:, 1, :]:
                            v.tensor_tensor(o, a, bb, ALU.min))
                        vop(lambda v, o=htmp[:, 2, :], a=habs[:, 2, :], bb=habs[:, 3, :]:
                            v.tensor_tensor(o, a, bb, ALU.max))
                        vop(lambda v, o=htmp[:, 3, :], a=habs[:, 2, :], bb=habs[:, 3, :]:
                            v.tensor_tensor(o, a, bb, ALU.min), inc="ve")
                        P.wait("vector", "ve", P.cur("ve"))
                        vop(lambda v, o=htmp[:, 0, :], a=htmp[:, 0, :], bb=htmp[:, 2, :]:
                            v.tensor_tensor(o, a, bb, ALU.min))
                        vop(lambda v, o=htmp[:, 1, :], a=htmp[:, 1, :], bb=htmp[:, 3, :]:
                            v.tensor_tensor(o, a, bb, ALU.max), inc="ve")
                        P.wait("vector", "ve", P.cur("ve"))
                        vop(lambda v, o=htmp[:, 0, :], a=htmp[:, 0, :], bb=htmp[:, 1, :]:
                            v.tensor_tensor(o, a, bb, ALU.max), inc="ve")
                        P.wait("vector", "ve", P.cur("ve"))
                        for s4 in range(4):
                            vop(lambda v, o=habs[:, s4, :], a=habs[:, s4, :], bb=htmp[:, 0, :]:
                                v.tensor_tensor(o, a, bb, ALU.is_ge),
                                inc="ve" if s4 == 3 else None)
                        P.wait("vector", "ve", P.cur("ve"))
                        for s4 in range(4):
                            ic4 = c4 + 16 * s4
                            P.claim("vector", ("hspT", ic4))
                            sp = vop(lambda v, o=hspT[:, ic4, :], a=hbuf[:, s4, :],
                                     bb=habs[:, s4, :]:
                                     v.tensor_tensor(o, a, bb, ALU.mult), inc="ve")
                            P.use(("hbuf", s4), "ve", sp)
                            if s4 == 3:
                                for s5 in range(4):
                                    P.use(("spars", c4, s5), "ve", sp)
                    if idx + 2 < len(seq):
                        gu_load(idx + 2)

                if g + 1 < 4:
                    emit_F0(g + 1)

                wdl = {}
                NB8 = NIC // 8

                def wd_block_load(oq, ib):
                    parw = ib % 2
                    P.claim("sync", ("wdc", parw))
                    wdl[(oq, ib)] = P.dma(
                        "sync", lambda s, d=wdc[:, parw],
                        src=wd_b[ib * 8:(ib + 1) * 8, :, oq * 512:(oq + 1) * 512].rearrange(
                            "c p f -> p c f"):
                        s.dma_start(d, src), f"wd{parw}")

                wd_block_load(0, 0)
                wd_block_load(0, 1)
                dn_mm = {}
                dn_evac = {}
                for oq in range(4):
                    for ib in range(NB8):
                        parw = ib % 2
                        for ic in range(ib * 8, ib * 8 + 8):
                            P.waituses("tensor", ("spars", ic % 16, ic // 16))
                        P.wait("tensor", *wdl[(oq, ib)])
                        vv = None
                        for tt in range(4):
                            if ib == 0:
                                P.claim("tensor", ("PF", tt))
                            for j in range(8):
                                ic = ib * 8 + j
                                st0 = (ic == 0)
                                sp0 = (ic == NIC - 1)
                                vv = P.emit(
                                    "tensor",
                                    lambda te, o=PF[tt][:],
                                    w=hspT[:, ic, tt * 128:(tt + 1) * 128],
                                    r=wdc[:, parw, j], s0=st0, s1=sp0:
                                    te.matmul(o, w, r, start=s0, stop=s1),
                                    inc="pe" if (tt == 3 and j == 7) else None)
                        if ib == NB8 - 1:
                            for tt in range(4):
                                dn_mm[(oq, tt)] = vv
                        P.use(("wdc", parw), "pe", vv)
                        nxt = (oq, ib + 2) if ib + 2 < NB8 else \
                              ((oq + 1, ib + 2 - NB8) if oq + 1 < 4 else None)
                        if nxt is not None:
                            wd_block_load(*nxt)
                    if oq == 3:
                        for icx in range(NIC):
                            P.use(("hspT", icx), "pe", P.cur("pe"))
                    for tt in range(4):
                        P.wait("scalar", "pe", dn_mm[(oq, tt)])
                        if oq == 0:
                            P.claim("scalar", ("ffo", tt))
                        ev = P.emit(
                            "scalar",
                            lambda sc, o=ffo[:, tt, oq * 512:(oq + 1) * 512], i=PF[tt][:]:
                            sc.activation(o, i, AF.Identity), inc="se")
                        P.use(("PF", tt), "se", ev)
                        dn_evac[(oq, tt)] = ev

                for tl in range(4):
                    t = g * 4 + tl
                    part = tl % 2
                    P.wait("scalar", "se", dn_evac[(3, tl)])
                    P.claim("scalar", ("stat", 0))
                    P.claim("scalar", "dumpN")
                    v1 = P.emit("scalar", lambda sc, o=dumpN[:], i=ffo[:, tl, :], a=g_sum[:, 0:1]:
                                sc.activation(o, i, AF.Identity, accum_out=a), inc="se")
                    P.wait("scalar", "se", v1)
                    v2 = P.emit("scalar", lambda sc, o=dumpN[:], i=ffo[:, tl, :], a=g_ssq[:, 0:1]:
                                sc.activation(o, i, AF.Square, accum_out=a), inc="se")
                    P.use("dumpN", "se", v2)
                    P.wait("vector", "se", v2)
                    u1 = P.emit("vector", lambda v, o=g_mean[:, 0:1], i=g_sum[:, 0:1]:
                                v.tensor_scalar_mul(o, i, INVD), inc="ve")
                    P.wait("vector", "ve", u1)
                    u2 = P.emit("vector", lambda v, o=g_var[:, 0:1], m=g_mean[:, 0:1]:
                                v.tensor_tensor(o, m, m, ALU.mult), inc="ve")
                    P.wait("vector", "ve", u2)
                    u3 = P.emit("vector", lambda v, o=g_var[:, 0:1], i=g_ssq[:, 0:1],
                                m=g_var[:, 0:1]:
                                v.tensor_scalar(o, i, INVD, m, ALU.mult, ALU.subtract), inc="ve")
                    P.wait("scalar", "ve", u3)
                    u4 = P.emit("scalar", lambda sc, o=g_std[:, 0:1], i=g_var[:, 0:1]:
                                sc.activation(o, i, AF.Sqrt, bias=LNEPS), inc="se")
                    P.wait("vector", "se", u4)
                    u5 = P.emit("vector", lambda v, o=g_rstd[:, 0:1], i=g_std[:, 0:1]:
                                v.reciprocal(o, i), inc="ve")
                    P.wait("vector", "ve", u5)
                    P.claim("vector", ("lnn", part))
                    u6 = P.emit("vector", lambda v, o=lnn[:, part, :], i=ffo[:, tl, :],
                                m=g_mean[:, 0:1], r=g_rstd[:, 0:1]:
                                v.tensor_scalar(o, i, m, r, ALU.subtract, ALU.mult), inc="ve")
                    P.use(("stat", 0), "ve", u6)
                    P.use(("ffo", tl), "ve", u6)
                    P.wait("gpsimd", *spillw[t])
                    P.wait("gpsimd", "ve", u6)
                    acsn, acv = P.dma("gpsimd",
                                      lambda gg, d=lnn[:, part, :], s=xp_s[t]:
                                      gg.dma_start(d, s, accum_op=ALU.add), f"ac{part}")
                    P.wait("gpsimd", acsn, acv)
                    yosn, ov = P.dma("gpsimd",
                                     lambda gg, d=y_out[b, t * 128:(t + 1) * 128, :], s=lnn[:, part, :]:
                                     gg.dma_start(d, s), f"yo{part}")
                    P.use(("lnn", part), yosn, ov)
                    out_dma_vals.append((yosn, ov))

        P.fence()
    except _StopBuild:
        pass

    for sn, eng in list(P.dmastreams.items()):
        P.wait(eng, sn, P.cur(sn))
    P.finalize()
    return nc


_CACHE = {}


def _get_nc():
    if "nc" not in _CACHE:
        _CACHE["nc"] = build_program()
    return _CACHE["nc"]


def _prep_weights(wq, wk, wv, wo, w_gate, w_up, w_down):
    def attn_tile(w):
        return np.ascontiguousarray(
            np.asarray(w).reshape(NHC, 128, NHC, 128).transpose(0, 3, 2, 1))

    wq_t = attn_tile(wq)
    wk_t = attn_tile(wk)
    wv_t = attn_tile(wv)
    wo_t = np.ascontiguousarray(np.asarray(wo).T.reshape(NHC, 128, D))
    wg_p = np.asarray(w_gate).reshape(D, DI // 4, 4).transpose(0, 2, 1).reshape(D, DI)
    wu_p = np.asarray(w_up).reshape(D, DI // 4, 4).transpose(0, 2, 1).reshape(D, DI)
    wd_p = np.asarray(w_down).reshape(DI // 4, 4, D).transpose(1, 0, 2).reshape(DI, D)
    wg_t = np.ascontiguousarray(wg_p.reshape(D, NIC, 128).transpose(1, 0, 2)
                                .reshape(NIC, NHC, 128, 128).transpose(0, 2, 1, 3))
    wu_t = np.ascontiguousarray(wu_p.reshape(D, NIC, 128).transpose(1, 0, 2)
                                .reshape(NIC, NHC, 128, 128).transpose(0, 2, 1, 3))
    wd_t = np.ascontiguousarray(wd_p.reshape(NIC, 128, D))
    return dict(wq_t=wq_t, wk_t=wk_t, wv_t=wv_t, wo_t=wo_t,
                wg_t=wg_t, wu_t=wu_t, wd_t=wd_t)


def kernel(L0_attn_hidden_states, L0_attn_wq, L0_attn_wk, L0_attn_wv, L0_attn_wo,
           L0_ffn_w_gate, L0_ffn_w_up, L0_ffn_w_down):
    x = np.asarray(L0_attn_hidden_states)
    wdict = _prep_weights(L0_attn_wq, L0_attn_wk, L0_attn_wv, L0_attn_wo,
                          L0_ffn_w_gate, L0_ffn_w_up, L0_ffn_w_down)
    bf = ml_dtypes.bfloat16
    consts = dict(
        c_id=np.eye(128, dtype=bf),
        c_mask=np.ascontiguousarray(np.tril(np.ones((128, 128), np.float32)).T.astype(bf)),
        c_onesc=np.ones((128, 1), bf),
        c_onesr=np.ones((1, 128), bf),
    )
    nc = _get_nc()
    in_maps = []
    for c in range(NCORES):
        m = dict(x=np.ascontiguousarray(x[c * NB:(c + 1) * NB]))
        m.update(wdict)
        m.update(consts)
        in_maps.append(m)
    res = run_bass_kernel_spmd(nc, in_maps, core_ids=list(range(NCORES)))
    out = np.concatenate([r["y"] for r in res.results], axis=0)
    return out


# version 10
# speedup vs baseline: 1.0357x; 1.0357x over previous
import math
import os
import numpy as np
import ml_dtypes
from contextlib import ExitStack

import concourse.bass as bass
import concourse.mybir as mybir
from concourse.bass_utils import run_bass_kernel_spmd

F32 = mybir.dt.float32
BF16 = mybir.dt.bfloat16
AF = mybir.ActivationFunctionType
ALU = mybir.AluOpType
AX = mybir.AxisListType

NCORES = 8
NB = 2
S = 2048
D = 2048
NT = S // 128
NH = 16
HD = 128
DI = 8192
NIC = DI // 128
NHC = D // 128
SCALE = 1.0 / math.sqrt(HD)
EXPC = 10.0
LNEPS = 1e-5
INVD = 1.0 / D

ENGINES = ["tensor", "vector", "scalar", "gpsimd", "sync"]
KPHASES = int(os.environ.get('KPHASES', '99'))
KF_NOMASK = os.environ.get('KF_NOMASK') == '1'
KF_NOPV = os.environ.get('KF_NOPV') == '1'
KF_NOZ = os.environ.get('KF_NOZ') == '1'
KF_NOEXP = os.environ.get('KF_NOEXP') == '1'
KF_NOST = os.environ.get('KF_NOST') == '1'
KF_QUADS = int(os.environ.get('KF_QUADS', '4'))
KF_HEADS = int(os.environ.get('KF_HEADS', str(NH)))


class _StopBuild(Exception):
    pass


class Prog:

    def __init__(self, nc):
        self.nc = nc
        self.streams = {e: [] for e in ENGINES}
        self.sems = {}
        self.ctx = ExitStack()
        self.bufuse = {}
        self.dmastreams = {}
        self.watermark = {}

    def sem(self, name):
        if name not in self.sems:
            h = self.ctx.enter_context(self.nc.semaphore(name))
            self.sems[name] = [h, 0]
        return self.sems[name]

    def dma(self, engine, fn, stream):
        name = "D_" + stream
        self.dmastreams[name] = engine
        return name, self.emit(engine, fn, inc=name, inc_by=16)

    def emit(self, engine, fn, inc=None, inc_by=1):
        rec = None
        val = None
        if inc is not None:
            s = self.sem(inc)
            s[1] += inc_by
            rec = (s[0], inc_by)
            val = s[1]
        self.streams[engine].append((fn, rec))
        return val

    def wait(self, engine, semname, value):
        if value is None or value <= 0:
            return
        wm = self.watermark.get((engine, semname), 0)
        if value <= wm:
            return
        self.watermark[(engine, semname)] = value
        s = self.sem(semname)
        assert value <= s[1], f"wait {semname}>={value} but only {s[1]} recorded"
        self.streams[engine].append(
            (lambda eng, h=s[0], v=value: eng.wait_ge(h, v), None))

    def cur(self, semname):
        return self.sem(semname)[1]

    def use(self, key, semname, value):
        d = self.bufuse.setdefault(key, {})
        if value is not None and value > d.get(semname, 0):
            d[semname] = value

    def claim(self, engine, key):
        for semname, value in self.bufuse.get(key, {}).items():
            self.wait(engine, semname, value)
        self.bufuse[key] = {}

    def waituses(self, engine, key):
        for semname, value in self.bufuse.get(key, {}).items():
            self.wait(engine, semname, value)

    def fence(self, engines=("tensor", "vector", "scalar", "gpsimd", "sync")):
        name = "fb"
        for sn, eng in list(self.dmastreams.items()):
            self.wait(eng, sn, self.cur(sn))
        for e in engines:
            if e in ("tensor", "vector", "scalar"):
                self.streams[e].append((lambda eng: eng.drain(), None))
            self.emit(e, lambda eng: eng.nop(), inc=name)
        v = self.cur(name)
        for e in engines:
            self.wait(e, name, v)
        self.bufuse = {}

    def finalize(self):
        with self.nc.Block() as block:
            for e in ENGINES:
                lst = self.streams[e]
                if not lst:
                    continue

                def body(eng, lst=lst):
                    for fn, rec in lst:
                        inst = fn(eng)
                        if rec is not None:
                            inst.then_inc(rec[0], rec[1])

                getattr(block, e)(body)
        try:
            self.ctx.close()
        except AssertionError:
            pass


def build_program():
    nc = bass.Bass("TRN2", target_bir_lowering=False)

    x_in = nc.dram_tensor("x", [NB, S, D], F32, kind="ExternalInput").ap()
    wq_f = nc.dram_tensor("wq_t", [NHC, 128, NHC, 128], F32, kind="ExternalInput").ap()
    wk_f = nc.dram_tensor("wk_t", [NHC, 128, NHC, 128], F32, kind="ExternalInput").ap()
    wv_f = nc.dram_tensor("wv_t", [NHC, 128, NHC, 128], F32, kind="ExternalInput").ap()
    wo_f = nc.dram_tensor("wo_t", [NHC, 128, D], F32, kind="ExternalInput").ap()
    wg_f = nc.dram_tensor("wg_t", [NIC, 128, NHC, 128], F32, kind="ExternalInput").ap()
    wu_f = nc.dram_tensor("wu_t", [NIC, 128, NHC, 128], F32, kind="ExternalInput").ap()
    wd_f = nc.dram_tensor("wd_t", [NIC, 128, D], F32, kind="ExternalInput").ap()
    c_id = nc.dram_tensor("c_id", [128, 128], BF16, kind="ExternalInput").ap()
    c_mask = nc.dram_tensor("c_mask", [128, 128], BF16, kind="ExternalInput").ap()
    c_onesc = nc.dram_tensor("c_onesc", [128, 1], BF16, kind="ExternalInput").ap()
    c_onesr = nc.dram_tensor("c_onesr", [1, 128], BF16, kind="ExternalInput").ap()
    y_out = nc.dram_tensor("y", [NB, S, D], F32, kind="ExternalOutput").ap()

    wq_b = nc.dram_tensor("wq_b", [NHC, 128, NHC, 128], BF16, kind="Internal").ap()
    wk_b = nc.dram_tensor("wk_b", [NHC, 128, NHC, 128], BF16, kind="Internal").ap()
    wv_b = nc.dram_tensor("wv_b", [NHC, 128, NHC, 128], BF16, kind="Internal").ap()
    wo_b = nc.dram_tensor("wo_b", [NHC, 128, D], BF16, kind="Internal").ap()
    wg_b = nc.dram_tensor("wg_b", [NIC, 128, NHC, 128], BF16, kind="Internal").ap()
    wu_b = nc.dram_tensor("wu_b", [NIC, 128, NHC, 128], BF16, kind="Internal").ap()
    wd_b = nc.dram_tensor("wd_b", [NIC, 128, D], BF16, kind="Internal").ap()
    qt_s = nc.dram_tensor("qt_s", [NH, 128, S], BF16, kind="Internal").ap()
    kt_s = nc.dram_tensor("kt_s", [NH, 128, S], BF16, kind="Internal").ap()
    ot_s = nc.dram_tensor("ot_s", [4, 128, NH, 512], BF16, kind="Internal").ap()
    xp_s = nc.dram_tensor("xp_s", [NT, 128, D], F32, kind="Internal").ap()

    P = Prog(nc)
    E = P.ctx.enter_context

    g_id = E(nc.sbuf_tensor("g_id", [128, 128], BF16))
    g_mask = E(nc.sbuf_tensor("g_mask", [128, 128], BF16))
    g_onesc = E(nc.sbuf_tensor("g_onesc", [128, 1], BF16))
    g_onesr = E(nc.sbuf_tensor("g_onesr", [1, 128], BF16))
    g_onesrf = E(nc.sbuf_tensor("g_onesrf", [1, 128], F32))
    g_cpool = E(nc.sbuf_tensor("g_cpool", [128, 2], F32))
    NSTAT = 4
    g_sum = E(nc.sbuf_tensor("g_sum", [128, NSTAT], F32))
    g_ssq = E(nc.sbuf_tensor("g_ssq", [128, NSTAT], F32))
    g_mean = E(nc.sbuf_tensor("g_mean", [128, NSTAT], F32))
    g_var = E(nc.sbuf_tensor("g_var", [128, NSTAT], F32))
    g_std = E(nc.sbuf_tensor("g_std", [128, NSTAT], F32))
    g_rstd = E(nc.sbuf_tensor("g_rstd", [128, NSTAT], F32))

    consts = [(-EXPC, 0), (LNEPS, 1)]
    for cv, j in consts:
        nc.const_aps.register(F32, cv, g_cpool[:, j:j + 1])

    for cv, j in consts:
        P.emit("gpsimd", lambda g, a=g_cpool[:, j:j + 1], v=cv: g.memset(a, v), inc="init")
    P.emit("gpsimd", lambda g, a=g_onesrf[:]: g.memset(a, 1.0), inc="init")
    for dst, csrc in ((g_id, c_id), (g_mask, c_mask), (g_onesc, c_onesc), (g_onesr, c_onesr)):
        P.dma("gpsimd", lambda g, d=dst[:], s=csrc: g.dma_start(d, s), "cst")
    P.emit("gpsimd", lambda g: g.nop(), inc="init")
    v_init = P.cur("init")
    gv_init = P.cur("D_cst")
    for e in ("tensor", "vector", "scalar", "sync"):
        P.wait(e, "init", v_init)
        P.wait(e, "D_cst", gv_init)

    wdone = {}
    _castq = []

    def _mk_pieces():
        for name, wsrc, dst, parts in (("wq", wq_f, wq_b, 4), ("wk", wk_f, wk_b, 4),
                                       ("wv", wv_f, wv_b, 4), ("wv2", None, None, 0),
                                       ("wo", wo_f, wo_b, 4),
                                       ("wg", wg_f, wg_b, 16), ("wu", wu_f, wu_b, 16),
                                       ("wd", wd_f, wd_b, 16)):
            if parts == 0:
                continue
            n0 = wsrc.shape[0]
            step = n0 // parts
            for i in range(parts):
                _castq.append((name, wsrc[i * step:(i + 1) * step],
                               dst[i * step:(i + 1) * step], i == parts - 1))

    _mk_pieces()

    def pace_cast(n=1):
        for _ in range(n):
            if not _castq:
                return
            name, wsrc, dst, last = _castq.pop(0)
            P.wait("gpsimd", "pe", P.cur("pe"))
            sn, _ = P.dma("gpsimd",
                          lambda g, d=dst, s=wsrc: g.dma_start(d, s), "wr" + name)
            if last:
                wdone[name] = (sn, P.cur(sn))

    def emit_ln(x_ap, out_ap, spar, dump, dumpkey, ready=()):
        for sn, v in ready:
            P.wait("scalar", sn, v)
        P.claim("scalar", dumpkey)
        P.claim("scalar", ("stat", spar))
        v_id = P.emit("scalar", lambda sc, o=dump, i=x_ap, a=g_sum[:, spar:spar + 1]:
                      sc.activation(o, i, AF.Identity, accum_out=a), inc="se")
        P.wait("scalar", "se", v_id)
        v_se = P.emit("scalar", lambda sc, o=dump, i=x_ap, a=g_ssq[:, spar:spar + 1]:
                      sc.activation(o, i, AF.Square, accum_out=a), inc="se")
        P.use(dumpkey, "se", v_se)
        P.wait("vector", "se", v_se)
        v1 = P.emit("vector", lambda v, o=g_mean[:, spar:spar + 1], i=g_sum[:, spar:spar + 1]:
                    v.tensor_scalar_mul(o, i, INVD), inc="ve")
        P.wait("vector", "ve", v1)
        v2 = P.emit("vector", lambda v, o=g_var[:, spar:spar + 1], m=g_mean[:, spar:spar + 1]:
                    v.tensor_tensor(o, m, m, ALU.mult), inc="ve")
        P.wait("vector", "ve", v2)
        v3 = P.emit("vector", lambda v, o=g_var[:, spar:spar + 1], i=g_ssq[:, spar:spar + 1],
                    m=g_var[:, spar:spar + 1]:
                    v.tensor_scalar(o, i, INVD, m, ALU.mult, ALU.subtract), inc="ve")
        P.wait("scalar", "ve", v3)
        v4 = P.emit("scalar", lambda sc, o=g_std[:, spar:spar + 1], i=g_var[:, spar:spar + 1]:
                    sc.activation(o, i, AF.Sqrt, bias=LNEPS), inc="se")
        P.wait("vector", "se", v4)
        v5 = P.emit("vector", lambda v, o=g_rstd[:, spar:spar + 1], i=g_std[:, spar:spar + 1]:
                    v.reciprocal(o, i), inc="ve")
        P.wait("vector", "ve", v5)
        v6 = P.emit("vector", lambda v, o=out_ap, i=x_ap, m=g_mean[:, spar:spar + 1],
                    r=g_rstd[:, spar:spar + 1]:
                    v.tensor_scalar(o, i, m, r, ALU.subtract, ALU.mult), inc="ve")
        P.use(("stat", spar), "ve", v6)
        return v6

    out_dma_vals = []

    NBRUN = min(NB, int(os.environ.get('KNB', str(NB))))
    try:
      for b in range(NBRUN):
        with ExitStack() as sAV:
            EV = sAV.enter_context
            V = EV(nc.sbuf_tensor(f"V{b}", [128, NT, D], BF16))

            with ExitStack() as sA:
                EA = sA.enter_context
                lnT = EA(nc.sbuf_tensor(f"lnT{b}", [128, NHC, S], BF16))
                wstat = EA(nc.sbuf_tensor(f"wstat{b}", [128, 3, NHC, 128], BF16))
                xb = EA(nc.sbuf_tensor(f"xbA{b}", [128, 4, D], F32))
                lnb = EA(nc.sbuf_tensor(f"lnbA{b}", [128, 2, D], BF16))
                eq = EA(nc.sbuf_tensor(f"eqA{b}", [128, 2, D], BF16))
                dumpA = EA(nc.sbuf_tensor(f"dumpA{b}", [128, D], F32))
                PH = [EA(nc.psum_tensor(f"PH{b}{i}", [128, 1024], F32)) for i in range(2)]
                g_pt = [EA(nc.psum_tensor(f"ptA{b}{i}", [128, 1024], BF16)) for i in range(2)]

                lnT_done = 0
                INVD2 = INVD * INVD
                ldv = {}
                v_se = {}
                v_var = {}
                v_std = {}
                lv = {}

                def a1_ld(t):
                    sl = t % 4
                    P.claim("scalar", ("xbA", sl))
                    ldv[t] = P.dma("scalar",
                                   lambda g, d=xb[:, sl, :], s=x_in[b, t * 128:(t + 1) * 128, :]:
                                   g.dma_start(d, s), f"xA{sl}")

                def a1_s1(t):
                    sl = t % 4
                    P.wait("scalar", *ldv[t])
                    P.claim("scalar", ("stat", sl))
                    P.claim("scalar", "dumpA")
                    vid = P.emit("scalar", lambda sc, o=dumpA[:], i=xb[:, sl, :],
                                 a=g_sum[:, sl:sl + 1]:
                                 sc.activation(o, i, AF.Identity, accum_out=a), inc="se")
                    P.wait("scalar", "se", vid)
                    v_se[t] = P.emit("scalar", lambda sc, o=dumpA[:], i=xb[:, sl, :],
                                     a=g_ssq[:, sl:sl + 1]:
                                     sc.activation(o, i, AF.Square, accum_out=a), inc="se")
                    P.use("dumpA", "se", v_se[t])

                def a1_s2(t):
                    sl = t % 4
                    P.wait("vector", "se", v_se[t])
                    m1 = P.emit("vector", lambda v, o=g_var[:, sl:sl + 1], i=g_sum[:, sl:sl + 1],
                                s1=g_sum[:, sl:sl + 1]:
                                v.tensor_scalar(o, i, s1, INVD2, ALU.mult, ALU.mult), inc="ve")
                    P.emit("vector", lambda v, o=g_mean[:, sl:sl + 1], i=g_sum[:, sl:sl + 1]:
                           v.tensor_scalar_mul(o, i, INVD), inc="ve")
                    P.wait("vector", "ve", m1)
                    v_var[t] = P.emit("vector", lambda v, o=g_var[:, sl:sl + 1],
                                      i=g_ssq[:, sl:sl + 1], m=g_var[:, sl:sl + 1]:
                                      v.tensor_scalar(o, i, INVD, m, ALU.mult, ALU.subtract),
                                      inc="ve")

                def a1_s3(t):
                    sl = t % 4
                    P.wait("scalar", "ve", v_var[t])
                    v_std[t] = P.emit("scalar", lambda sc, o=g_std[:, sl:sl + 1],
                                      i=g_var[:, sl:sl + 1]:
                                      sc.activation(o, i, AF.Sqrt, bias=LNEPS), inc="se")

                def a1_s4(t):
                    sl = t % 4
                    P.wait("vector", "se", v_std[t])
                    r1 = P.emit("vector", lambda v, o=g_rstd[:, sl:sl + 1], i=g_std[:, sl:sl + 1]:
                                v.reciprocal(o, i), inc="ve")
                    P.wait("vector", "ve", r1)
                    P.claim("vector", ("lnbA", t % 2))
                    lv[t] = P.emit("vector", lambda v, o=lnb[:, t % 2, :], i=xb[:, sl, :],
                                   m=g_mean[:, sl:sl + 1], r=g_rstd[:, sl:sl + 1]:
                                   v.tensor_scalar(o, i, m, r, ALU.subtract, ALU.mult), inc="ve")
                    P.use(("xbA", sl), "ve", lv[t])
                    P.use(("stat", sl), "ve", lv[t])

                def a1_tr(t):
                    nonlocal lnT_done
                    P.wait("tensor", "ve", lv[t])
                    for grp in range(2):
                        P.claim("tensor", ("pt", grp))
                        tv = None
                        for j in range(8):
                            hc = grp * 8 + j
                            tv = P.emit(
                                "tensor",
                                lambda te, o=g_pt[grp][:, j * 128:(j + 1) * 128],
                                i=lnb[:, t % 2, hc * 128:(hc + 1) * 128]:
                                te.transpose(o, i, g_id[:]),
                                inc="pe" if j == 7 else None)
                        P.use(("lnbA", t % 2), "pe", tv)
                        P.wait("vector", "pe", tv)
                        ev = P.emit(
                            "vector",
                            lambda v, o=lnT[:, grp * 8:(grp + 1) * 8, t * 128:(t + 1) * 128],
                            i=g_pt[grp].ap().rearrange("p (c f) -> p c f", c=8):
                            v.tensor_copy(o, i), inc="ve")
                        P.use(("pt", grp), "ve", ev)
                        lnT_done = ev

                for t in range(4):
                    a1_ld(t)
                for step in range(NT + 3):
                    if b == 0 and step % 4 == 3:
                        pace_cast(1)
                    if step < NT:
                        a1_s1(step)
                    if 0 <= step - 1 < NT:
                        a1_s2(step - 1)
                    if 0 <= step - 2 < NT:
                        a1_s3(step - 2)
                    if 0 <= step - 3 < NT:
                        a1_s4(step - 3)
                        if step + 1 < NT:
                            a1_ld(step + 1)
                        a1_tr(step - 3)

                def qkv_phase(w_bf, spill_dst, wname, is_v):
                    P.wait("sync", *wdone[wname])
                    spills = {}
                    loadv = {}

                    def load(oc):
                        wpar = oc % 3
                        P.claim("sync", ("wstat", wpar))
                        loadv[oc] = P.dma(
                            "sync", lambda s, d=wstat[:, wpar], src=w_bf[oc]:
                            s.dma_start(d, src), f"ws{wpar}")

                    load(0)
                    load(1)
                    load(2)
                    for oc in range(NHC):
                        par = oc % 2
                        wpar = oc % 3
                        if b == 0 and oc % 4 == 1:
                            pace_cast(1)
                        P.wait("tensor", *loadv[oc])
                        P.wait("tensor", "ve", lnT_done)
                        mmv = [0, 0]
                        for sh in range(2):
                            P.claim("tensor", ("PH", sh))
                            for hc in range(NHC):
                                for half in range(2):
                                    st0 = (hc == 0)
                                    sp0 = (hc == NHC - 1)
                                    vv = P.emit(
                                        "tensor",
                                        lambda te, o=PH[sh][:, half * 512:(half + 1) * 512],
                                        w=wstat[:, wpar, hc, :],
                                        r=lnT[:, hc, sh * 1024 + half * 512:sh * 1024 + (half + 1) * 512],
                                        s0=st0, s1=sp0:
                                        te.matmul(o, w, r, start=s0, stop=s1),
                                        inc="pe" if (sp0 and half == 1) else None)
                            mmv[sh] = vv
                            P.use(("wstat", wpar), "pe", vv)
                            P.wait("vector", "pe", vv)
                            P.claim("vector", ("eq", par, sh))
                            ev = P.emit(
                                "vector",
                                lambda v, o=eq[:, par, sh * 1024:(sh + 1) * 1024], i=PH[sh][:]:
                                v.tensor_copy(o, i), inc="ve")
                            P.use(("PH", sh), "ve", ev)
                            if is_v:
                                P.wait("tensor", "ve", ev)
                                P.claim("tensor", ("pt", sh))
                                tv = None
                                for j in range(8):
                                    tv = P.emit(
                                        "tensor",
                                        lambda te, o=g_pt[sh][:, j * 128:(j + 1) * 128],
                                        i=eq[:, par, sh * 1024 + j * 128:sh * 1024 + (j + 1) * 128]:
                                        te.transpose(o, i, g_id[:]),
                                        inc="pe" if j == 7 else None)
                                P.use(("eq", par, sh), "pe", tv)
                                P.wait("vector", "pe", tv)
                                vev = P.emit(
                                    "vector",
                                    lambda v, o=V[:, sh * 8:(sh + 1) * 8, oc * 128:(oc + 1) * 128],
                                    i=g_pt[sh].ap().rearrange("p (c f) -> p c f", c=8):
                                    v.tensor_copy(o, i), inc="ve")
                                P.use(("pt", sh), "ve", vev)
                        if not is_v:
                            P.wait("sync", "ve", P.cur("ve"))
                            ssn, sv = P.dma(
                                "sync", lambda g, d=spill_dst[oc], s=eq[:, par, :]:
                                g.dma_start(d, s), f"sp{par}")
                            P.use(("eq", par, 0), ssn, sv)
                            P.use(("eq", par, 1), ssn, sv)
                            spills[ssn] = sv
                        if oc + 3 < NHC:
                            load(oc + 3)
                    return spills

                if KPHASES <= 1:
                    raise _StopBuild
                q_spill_done = qkv_phase(wq_b, qt_s, "wq", False)
                k_spill_done = qkv_phase(wk_b, kt_s, "wk", False)
                qkv_phase(wv_b, None, "wv", True)
                if KPHASES <= 2:
                    raise _StopBuild

            P.fence()

            with ExitStack() as sF:
                EF = sF.enter_context
                kT = EF(nc.sbuf_tensor(f"kT{b}", [128, NH, S], BF16))
                qTq = EF(nc.sbuf_tensor(f"qTq{b}", [128, NH, 512], BF16))
                ET = [EF(nc.sbuf_tensor(f"ET{b}{i}", [128, NT, 512], BF16)) for i in range(2)]
                otq = EF(nc.sbuf_tensor(f"otq{b}", [128, NH, 512], BF16))
                crow = EF(nc.sbuf_tensor(f"crow{b}", [1, 2, 512], BF16))
                bcs = EF(nc.sbuf_tensor(f"bcs{b}", [128, 512], F32))
                ST = [EF(nc.psum_tensor(f"ST{b}{i}", [128, 1024], F32)) for i in range(2)]
                OTp = [EF(nc.psum_tensor(f"OTp{b}{i}", [128, 512], F32)) for i in range(2)]
                Zp = EF(nc.psum_tensor(f"Zp{b}", [1, 512], F32))
                BCp = EF(nc.psum_tensor(f"BCp{b}", [128, 512], F32))

                ot_spill = {}
                for p in range(KF_QUADS):
                    qc0 = p * 512
                    for sn, vv in list(q_spill_done.items()) + list(k_spill_done.items()):
                        P.wait("sync", sn, vv)
                    P.claim("sync", "qTq")
                    for hh in range(NH):
                        qsn, qt_load = P.dma(
                            "sync", lambda s, d=qTq[:, hh, :], src=qt_s[hh, :, qc0:qc0 + 512]:
                            s.dma_start(d, src), "fq")
                    for hh in range(NH):
                        ksn, kt_load = P.dma(
                            "sync", lambda s, d=kT[:, hh, qc0:qc0 + 512],
                            src=kt_s[hh, :, qc0:qc0 + 512]:
                            s.dma_start(d, src), "fk")

                    kmax = 4 * p + 3
                    for h in range(KF_HEADS):
                        if b == 0:
                            pace_cast(1)
                        etb = ET[h % 2]
                        P.wait("tensor", "D_fq", qt_load)
                        P.wait("tensor", "D_fk", kt_load)
                        stv = {}
                        expv = {}
                        P.claim("scalar", ("ET", h % 2))
                        for kc0 in range(0, kmax + 1, 2):
                            stp = (kc0 // 2) % 2
                            P.claim("tensor", ("ST", stp))
                            offs = []
                            for kc in (kc0, kc0 + 1):
                                if kc > kmax:
                                    continue
                                off = max(kc - 4 * p, 0) * 128
                                offs.append((kc, off))
                                stv[kc] = P.emit(
                                    "tensor",
                                    lambda te, o=ST[stp][:, (kc % 2) * 512 + off:(kc % 2) * 512 + 512],
                                    w=kT[:, h, kc * 128:(kc + 1) * 128], r=qTq[:, h, off:512]:
                                    te.matmul(o, w, r, start=True, stop=True), inc="pe")
                                P.use("qTq", "pe", stv[kc])
                            last_mm = stv[offs[-1][0]]
                            P.wait("scalar", "pe", last_mm)
                            if len(offs) == 2 and offs[0][1] == 0 and offs[1][1] == 0:
                                ev = P.emit(
                                    "scalar",
                                    lambda sc, o=etb[:, kc0:kc0 + 2, :], i=ST[stp][:]:
                                    sc.activation(o, i, AF.Exp, bias=-EXPC, scale=SCALE),
                                    inc="se")
                                expv[kc0] = expv[kc0 + 1] = ev
                            else:
                                for kc, off in offs:
                                    expv[kc] = P.emit(
                                        "scalar",
                                        lambda sc, o=etb[:, kc, off:512],
                                        i=ST[stp][:, (kc % 2) * 512 + off:(kc % 2) * 512 + 512]:
                                        sc.activation(o, i, AF.Exp, bias=-EXPC, scale=SCALE),
                                        inc="se")
                            P.use(("ST", stp), "se", expv[offs[-1][0]])
                        mv = None
                        for j in range(4 if not KF_NOMASK else 0):
                            kc = 4 * p + j
                            off = j * 128
                            P.wait("vector", "se", expv[kc])
                            mv = P.emit(
                                "vector",
                                lambda v, a=etb[:, kc, off:off + 128]:
                                v.tensor_tensor(a, a, g_mask[:], ALU.mult), inc="ve")
                        if KF_NOPV:
                            continue
                        P.wait("tensor", "se", expv[kmax])
                        P.wait("tensor", "ve", mv)
                        otp = OTp[h % 2]
                        P.claim("tensor", ("OTp", h % 2))
                        P.claim("tensor", "Zp")
                        pvz = None
                        for kc in range(kmax + 1):
                            off = max(kc - 4 * p, 0) * 128
                            P.emit(
                                "tensor",
                                lambda te, o=otp[:, off:512], w=V[:, kc, h * 128:(h + 1) * 128],
                                r=etb[:, kc, off:512], s0=(kc == 0), s1=(kc == kmax):
                                te.matmul(o, w, r, start=s0, stop=s1))
                        for kc in range(kmax + 1):
                            off = max(kc - 4 * p, 0) * 128
                            pvz = P.emit(
                                "tensor",
                                lambda te, o=Zp[0:1, off:512], w=g_onesc[:],
                                r=etb[:, kc, off:512], s0=(kc == 0), s1=(kc == kmax):
                                te.matmul(o, w, r, start=s0, stop=s1), inc="pe")
                        P.use(("ET", h % 2), "pe", pvz)
                        if KF_NOZ:
                            P.wait("vector", "pe", pvz)
                            P.claim("vector", ("otq", h))
                            oev = P.emit("vector", lambda v, o=otq[:, h, :], i=otp[:]:
                                         v.tensor_copy(o, i), inc="ve")
                            P.use(("OTp", h % 2), "ve", oev)
                        else:
                            P.wait("vector", "pe", pvz)
                            P.claim("vector", ("crow", h % 2))
                            def _recipb(v, o=crow[0:1, h % 2, :], i=Zp[0:1, :], _nc=nc):
                                with _nc.allow_low_precision("crow bf16: per-row scale"):
                                    return v.reciprocal(o, i)
                            rv = P.emit("vector", _recipb, inc="ve")
                            P.use("Zp", "ve", rv)
                            P.wait("tensor", "ve", rv)
                            P.claim("tensor", "BCp")
                            bcv = P.emit("tensor",
                                         lambda te, o=BCp[:], w=g_onesr[:], r=crow[0:1, h % 2, :]:
                                         te.matmul(o, w, r, start=True, stop=True), inc="pe")
                            P.use(("crow", h % 2), "pe", bcv)
                            P.wait("vector", "pe", bcv)
                            P.claim("vector", "bcs")
                            cv = P.emit("vector", lambda v, o=bcs[:], i=BCp[:]:
                                        v.tensor_copy(o, i), inc="ve")
                            P.use("BCp", "ve", cv)
                            P.claim("vector", ("otq", h))
                            P.wait("vector", "ve", cv)
                            oev = P.emit("vector", lambda v, o=otq[:, h, :], i=otp[:], c=bcs[:]:
                                         v.tensor_tensor(o, i, c, ALU.mult), inc="ve")
                            P.use(("OTp", h % 2), "ve", oev)
                            P.use("bcs", "ve", oev)
                    P.wait("gpsimd", "ve", P.cur("ve"))
                    osn, sv = P.dma("gpsimd", lambda g, d=ot_s[p], s=otq[:]:
                                    g.dma_start(d, s), "os")
                    for h in range(NH):
                        P.use(("otq", h), osn, sv)
                    ot_spill[p] = (osn, sv)

            P.fence()
            if KPHASES <= 3:
                raise _StopBuild

        with ExitStack() as sW:
            EW = sW.enter_context
            wo_res = EW(nc.sbuf_tensor(f"wo_res{b}", [128, NHC, D], BF16))
            otq_wo = EW(nc.sbuf_tensor(f"otq_wo{b}", [128, 2, NH, 512], BF16))
            xbw = EW(nc.sbuf_tensor(f"xbW{b}", [128, 2, D], F32))
            attn_t = EW(nc.sbuf_tensor(f"attn_t{b}", [128, D], F32))
            lnw = EW(nc.sbuf_tensor(f"lnw{b}", [128, D], F32))
            xpo = EW(nc.sbuf_tensor(f"xpoW{b}", [128, 2, D], F32))
            dumpW = EW(nc.sbuf_tensor(f"dumpW{b}", [128, D], F32))
            PW = [EW(nc.psum_tensor(f"PW{b}{i}", [128, 1024], F32)) for i in range(2)]

            P.wait("sync", *wdone["wo"])
            for fc in range(NHC):
                wsn, wo_load = P.dma("sync", lambda s, d=wo_res[:, fc, :], src=wo_b[fc]:
                                     s.dma_start(d, src), "wo")
            spillw = [0] * NT
            for t in range(NT):
                p = t // 4
                parq = p % 2
                if b == 0:
                    pace_cast(1)
                if t % 4 == 0:
                    P.wait("sync", *ot_spill[p])
                    P.claim("sync", ("otq_wo", parq))
                    oqsn, oq_load = P.dma(
                        "sync", lambda s, d=otq_wo[:, parq], src=ot_s[p]:
                        s.dma_start(d, src), f"oq{parq}")
                P.claim("scalar", ("xbW", t % 2))
                xwsn, xv = P.dma(
                    "scalar", lambda g, d=xbw[:, t % 2, :], s=x_in[b, t * 128:(t + 1) * 128, :]:
                    g.dma_start(d, s), f"xW{t % 2}")
                P.wait("tensor", "D_wo", wo_load)
                P.wait("tensor", oqsn, oq_load)
                P.claim("scalar", ("stat", 0))
                P.claim("scalar", ("stat", 1))
                last_oq_use = None
                for oh in range(2):
                    P.claim("tensor", ("PW", oh))
                    vv = None
                    for fc in range(NHC):
                        for half in range(2):
                            st0 = (fc == 0)
                            sp0 = (fc == NHC - 1)
                            vv = P.emit(
                                "tensor",
                                lambda te, o=PW[oh][:, half * 512:(half + 1) * 512],
                                w=otq_wo[:, parq, fc, (t % 4) * 128:(t % 4) * 128 + 128],
                                r=wo_res[:, fc, oh * 1024 + half * 512:oh * 1024 + (half + 1) * 512],
                                s0=st0, s1=sp0:
                                te.matmul(o, w, r, start=s0, stop=s1),
                                inc="pe" if (sp0 and half == 1) else None)
                    last_oq_use = vv
                    P.wait("scalar", "pe", vv)
                    if oh == 0:
                        P.claim("scalar", ("attn_t",))
                    ev = P.emit(
                        "scalar",
                        lambda sc, o=attn_t[:, oh * 1024:(oh + 1) * 1024], i=PW[oh][:],
                        a=g_sum[:, oh:oh + 1]:
                        sc.activation(o, i, AF.Identity, accum_out=a), inc="se")
                    P.use(("PW", oh), "se", ev)
                P.use(("otq_wo", parq), "pe", last_oq_use)
                P.claim("scalar", ("dumpW",))
                P.wait("scalar", "se", P.cur("se"))
                v_sq = P.emit("scalar", lambda sc, o=dumpW[:], i=attn_t[:], a=g_ssq[:, 0:1]:
                              sc.activation(o, i, AF.Square, accum_out=a), inc="se")
                P.use(("dumpW",), "se", v_sq)
                P.wait("vector", "se", v_sq)
                w1 = P.emit("vector", lambda v, o=g_sum[:, 0:1], a=g_sum[:, 0:1], bb=g_sum[:, 1:2]:
                            v.tensor_tensor(o, a, bb, ALU.add), inc="ve")
                P.wait("vector", "ve", w1)
                w2 = P.emit("vector", lambda v, o=g_mean[:, 0:1], i=g_sum[:, 0:1]:
                            v.tensor_scalar_mul(o, i, INVD), inc="ve")
                P.wait("vector", "ve", w2)
                w3 = P.emit("vector", lambda v, o=g_var[:, 0:1], m=g_mean[:, 0:1]:
                            v.tensor_tensor(o, m, m, ALU.mult), inc="ve")
                P.wait("vector", "ve", w3)
                w4 = P.emit("vector", lambda v, o=g_var[:, 0:1], i=g_ssq[:, 0:1], m=g_var[:, 0:1]:
                            v.tensor_scalar(o, i, INVD, m, ALU.mult, ALU.subtract), inc="ve")
                P.wait("scalar", "ve", w4)
                w5 = P.emit("scalar", lambda sc, o=g_std[:, 0:1], i=g_var[:, 0:1]:
                            sc.activation(o, i, AF.Sqrt, bias=LNEPS), inc="se")
                P.wait("vector", "se", w5)
                w6 = P.emit("vector", lambda v, o=g_rstd[:, 0:1], i=g_std[:, 0:1]:
                            v.reciprocal(o, i), inc="ve")
                P.wait("vector", "ve", w6)
                P.claim("vector", ("lnw",))
                w7 = P.emit("vector", lambda v, o=lnw[:], i=attn_t[:], m=g_mean[:, 0:1],
                            r=g_rstd[:, 0:1]:
                            v.tensor_scalar(o, i, m, r, ALU.subtract, ALU.mult), inc="ve")
                P.use(("attn_t",), "ve", w7)
                P.use(("stat", 0), "ve", w7)
                P.use(("stat", 1), "ve", w7)
                P.wait("vector", xwsn, xv)
                P.wait("vector", "ve", w7)
                P.claim("vector", ("xpo", t % 2))
                w8 = P.emit("vector", lambda v, o=xpo[:, t % 2, :], i=lnw[:], xx=xbw[:, t % 2, :]:
                            v.tensor_tensor(o, i, xx, ALU.add), inc="ve")
                P.use(("lnw",), "ve", w8)
                P.use(("xbW", t % 2), "ve", w8)
                P.wait("gpsimd", "ve", w8)
                xssn, sv = P.dma("gpsimd", lambda g, d=xp_s[t], s=xpo[:, t % 2, :]:
                                 g.dma_start(d, s), f"xs{t % 2}")
                P.use(("xpo", t % 2), xssn, sv)
                spillw[t] = (xssn, sv)

        P.fence()
        if KPHASES <= 4:
            raise _StopBuild

        with ExitStack() as sN:
            EN = sN.enter_context
            lnT2 = EN(nc.sbuf_tensor(f"lnT2{b}", [128, 2, NHC, 512], BF16))
            hspT = EN(nc.sbuf_tensor(f"hspT{b}", [128, NIC, 512], BF16))
            hbuf = EN(nc.sbuf_tensor(f"hbuf{b}", [128, 4, 512], BF16))
            habs = EN(nc.sbuf_tensor(f"habs{b}", [128, 4, 512], BF16))
            htmp = EN(nc.sbuf_tensor(f"htmp{b}", [128, 4, 512], BF16))
            sg = EN(nc.sbuf_tensor(f"sg{b}", [128, 2, 512], F32))
            wgu = EN(nc.sbuf_tensor(f"wgu{b}", [128, 2, 2, NHC, 128], BF16))
            wdc = EN(nc.sbuf_tensor(f"wdc{b}", [128, 2, 8, 512], BF16))
            xbn = EN(nc.sbuf_tensor(f"xbN{b}", [128, 2, D], F32))
            lnbn = EN(nc.sbuf_tensor(f"lnbN{b}", [128, 2, D], BF16))
            ffo = EN(nc.sbuf_tensor(f"ffo{b}", [128, 4, D], BF16))
            lnn = EN(nc.sbuf_tensor(f"lnn{b}", [128, 2, D], F32))
            dumpN = EN(nc.sbuf_tensor(f"dumpN{b}", [128, D], BF16))
            PF = [EN(nc.psum_tensor(f"PF{b}{i}", [128, 512], F32)) for i in range(4)]
            g_pt = [EN(nc.psum_tensor(f"ptN{b}{i}", [128, 1024], BF16)) for i in range(2)]

            P.wait("sync", *wdone["wg"])
            P.wait("sync", *wdone["wu"])
            P.wait("sync", *wdone["wd"])

            ln2_done = {}

            def emit_F0(g):
                gp = g % 2
                P.claim("vector", ("lnT2", gp))
                for tl in range(4):
                    t = g * 4 + tl
                    par = tl % 2
                    P.wait("scalar", *spillw[t])
                    P.claim("scalar", ("xbN", par))
                    xnsn, xv = P.dma("scalar", lambda gg, d=xbn[:, par, :], s=xp_s[t]:
                                     gg.dma_start(d, s), f"xN{par}")
                    P.claim("vector", ("lnbN", par))
                    lv = emit_ln(xbn[:, par, :], lnbn[:, par, :], 2 + par, dumpN[:], "dumpN",
                                 ready=[(xnsn, xv)])
                    P.use(("xbN", par), "ve", lv)
                    P.wait("tensor", "ve", lv)
                    for grp in range(2):
                        P.claim("tensor", ("pt", grp))
                        tv = None
                        for j in range(8):
                            hc = grp * 8 + j
                            tv = P.emit(
                                "tensor",
                                lambda te, o=g_pt[grp][:, j * 128:(j + 1) * 128],
                                i=lnbn[:, par, hc * 128:(hc + 1) * 128]:
                                te.transpose(o, i, g_id[:]),
                                inc="pe" if j == 7 else None)
                        P.use(("lnbN", par), "pe", tv)
                        P.wait("vector", "pe", tv)
                        ev = P.emit(
                            "vector",
                            lambda v, o=lnT2[:, gp, grp * 8:(grp + 1) * 8, tl * 128:(tl + 1) * 128],
                            i=g_pt[grp].ap().rearrange("p (c f) -> p c f", c=8):
                            v.tensor_copy(o, i), inc="ve")
                        P.use(("pt", grp), "ve", ev)
                        ln2_done[g] = ev

            emit_F0(0)
            for g in range(4):
                gp = g % 2
                ln2_last = ln2_done[g]

                seq = [(c4, sub) for c4 in range(16) for sub in range(4)]
                guload = {}

                def gu_load(idx):
                    c4, sub = seq[idx]
                    parb = sub % 2
                    for pj, wsrc in enumerate((wg_b, wu_b)):
                        ic = c4 + 16 * sub
                        P.claim("sync", ("wgu", parb, pj))
                        guload[(c4, sub, pj)] = P.dma(
                            "sync", lambda s, d=wgu[:, parb, pj], src=wsrc[ic]:
                            s.dma_start(d, src), f"wg{parb}{pj}")

                gu_load(0)
                gu_load(1)
                for idx, (c4, sub) in enumerate(seq):
                    ic = c4 + 16 * sub
                    parb = sub % 2
                    for pj in range(2):
                        bank = 2 * parb + pj
                        P.wait("tensor", *guload[(c4, sub, pj)])
                        P.wait("tensor", "ve", ln2_last)
                        P.claim("tensor", ("PF", bank))
                        vv = None
                        for hc in range(NHC):
                            st0 = (hc == 0)
                            sp0 = (hc == NHC - 1)
                            vv = P.emit(
                                "tensor",
                                lambda te, o=PF[bank][:], w=wgu[:, parb, pj, hc, :],
                                r=lnT2[:, gp, hc, :], s0=st0, s1=sp0:
                                te.matmul(o, w, r, start=s0, stop=s1),
                                inc="pe" if sp0 else None)
                        P.use(("wgu", parb, pj), "pe", vv)
                        P.use(("lnT2", gp), "pe", vv)
                        if pj == 0:
                            gate_mm = vv
                        else:
                            up_mm = vv
                    P.wait("scalar", "pe", gate_mm)
                    P.claim("scalar", ("sg", parb))
                    sl = P.emit("scalar", lambda sc, o=sg[:, parb, :], i=PF[2 * parb][:]:
                                sc.activation(o, i, AF.Silu), inc="se")
                    P.use(("PF", 2 * parb), "se", sl)
                    P.wait("vector", "se", sl)
                    P.wait("vector", "pe", up_mm)
                    P.claim("vector", ("hbuf", sub))
                    hv = P.emit("vector", lambda v, o=hbuf[:, sub, :], i=sg[:, parb, :],
                                u=PF[2 * parb + 1][:]:
                                v.tensor_tensor(o, i, u, ALU.mult), inc="ve")
                    P.use(("sg", parb), "ve", hv)
                    P.use(("PF", 2 * parb + 1), "ve", hv)
                    if sub == 3:
                        def vop(fn, inc=None):
                            return P.emit("vector", fn, inc=inc)
                        P.wait("vector", "ve", hv)
                        for s4 in range(4):
                            vop(lambda v, o=habs[:, s4, :], i=hbuf[:, s4, :]:
                                v.tensor_tensor(o, i, i, ALU.mult),
                                inc="ve" if s4 == 3 else None)
                        P.wait("vector", "ve", P.cur("ve"))
                        vop(lambda v, o=htmp[:, 0, :], a=habs[:, 0, :], bb=habs[:, 1, :]:
                            v.tensor_tensor(o, a, bb, ALU.max))
                        vop(lambda v, o=htmp[:, 1, :], a=habs[:, 0, :], bb=habs[:, 1, :]:
                            v.tensor_tensor(o, a, bb, ALU.min))
                        vop(lambda v, o=htmp[:, 2, :], a=habs[:, 2, :], bb=habs[:, 3, :]:
                            v.tensor_tensor(o, a, bb, ALU.max))
                        vop(lambda v, o=htmp[:, 3, :], a=habs[:, 2, :], bb=habs[:, 3, :]:
                            v.tensor_tensor(o, a, bb, ALU.min), inc="ve")
                        P.wait("vector", "ve", P.cur("ve"))
                        vop(lambda v, o=htmp[:, 0, :], a=htmp[:, 0, :], bb=htmp[:, 2, :]:
                            v.tensor_tensor(o, a, bb, ALU.min))
                        vop(lambda v, o=htmp[:, 1, :], a=htmp[:, 1, :], bb=htmp[:, 3, :]:
                            v.tensor_tensor(o, a, bb, ALU.max), inc="ve")
                        P.wait("vector", "ve", P.cur("ve"))
                        vop(lambda v, o=htmp[:, 0, :], a=htmp[:, 0, :], bb=htmp[:, 1, :]:
                            v.tensor_tensor(o, a, bb, ALU.max), inc="ve")
                        P.wait("vector", "ve", P.cur("ve"))
                        for s4 in range(4):
                            vop(lambda v, o=habs[:, s4, :], a=habs[:, s4, :], bb=htmp[:, 0, :]:
                                v.tensor_tensor(o, a, bb, ALU.is_ge),
                                inc="ve" if s4 == 3 else None)
                        P.wait("vector", "ve", P.cur("ve"))
                        for s4 in range(4):
                            ic4 = c4 + 16 * s4
                            P.claim("vector", ("hspT", ic4))
                            sp = vop(lambda v, o=hspT[:, ic4, :], a=hbuf[:, s4, :],
                                     bb=habs[:, s4, :]:
                                     v.tensor_tensor(o, a, bb, ALU.mult), inc="ve")
                            P.use(("hbuf", s4), "ve", sp)
                            if s4 == 3:
                                for s5 in range(4):
                                    P.use(("spars", c4, s5), "ve", sp)
                    if idx + 2 < len(seq):
                        gu_load(idx + 2)

                if g + 1 < 4:
                    emit_F0(g + 1)

                wdl = {}
                NB8 = NIC // 8

                def wd_block_load(oq, ib):
                    parw = ib % 2
                    P.claim("sync", ("wdc", parw))
                    wdl[(oq, ib)] = P.dma(
                        "sync", lambda s, d=wdc[:, parw],
                        src=wd_b[ib * 8:(ib + 1) * 8, :, oq * 512:(oq + 1) * 512].rearrange(
                            "c p f -> p c f"):
                        s.dma_start(d, src), f"wd{parw}")

                wd_block_load(0, 0)
                wd_block_load(0, 1)
                dn_mm = {}
                dn_evac = {}
                for oq in range(4):
                    for ib in range(NB8):
                        parw = ib % 2
                        for ic in range(ib * 8, ib * 8 + 8):
                            P.waituses("tensor", ("spars", ic % 16, ic // 16))
                        P.wait("tensor", *wdl[(oq, ib)])
                        vv = None
                        for tt in range(4):
                            if ib == 0:
                                P.claim("tensor", ("PF", tt))
                            for j in range(8):
                                ic = ib * 8 + j
                                st0 = (ic == 0)
                                sp0 = (ic == NIC - 1)
                                vv = P.emit(
                                    "tensor",
                                    lambda te, o=PF[tt][:],
                                    w=hspT[:, ic, tt * 128:(tt + 1) * 128],
                                    r=wdc[:, parw, j], s0=st0, s1=sp0:
                                    te.matmul(o, w, r, start=s0, stop=s1),
                                    inc="pe" if (tt == 3 and j == 7) else None)
                        if ib == NB8 - 1:
                            for tt in range(4):
                                dn_mm[(oq, tt)] = vv
                        P.use(("wdc", parw), "pe", vv)
                        nxt = (oq, ib + 2) if ib + 2 < NB8 else \
                              ((oq + 1, ib + 2 - NB8) if oq + 1 < 4 else None)
                        if nxt is not None:
                            wd_block_load(*nxt)
                    if oq == 3:
                        for icx in range(NIC):
                            P.use(("hspT", icx), "pe", P.cur("pe"))
                    for tt in range(4):
                        P.wait("scalar", "pe", dn_mm[(oq, tt)])
                        if oq == 0:
                            P.claim("scalar", ("ffo", tt))
                        ev = P.emit(
                            "scalar",
                            lambda sc, o=ffo[:, tt, oq * 512:(oq + 1) * 512], i=PF[tt][:]:
                            sc.activation(o, i, AF.Identity), inc="se")
                        P.use(("PF", tt), "se", ev)
                        dn_evac[(oq, tt)] = ev

                for tl in range(4):
                    t = g * 4 + tl
                    part = tl % 2
                    P.wait("scalar", "se", dn_evac[(3, tl)])
                    P.claim("scalar", ("stat", 0))
                    P.claim("scalar", "dumpN")
                    v1 = P.emit("scalar", lambda sc, o=dumpN[:], i=ffo[:, tl, :], a=g_sum[:, 0:1]:
                                sc.activation(o, i, AF.Identity, accum_out=a), inc="se")
                    P.wait("scalar", "se", v1)
                    v2 = P.emit("scalar", lambda sc, o=dumpN[:], i=ffo[:, tl, :], a=g_ssq[:, 0:1]:
                                sc.activation(o, i, AF.Square, accum_out=a), inc="se")
                    P.use("dumpN", "se", v2)
                    P.wait("vector", "se", v2)
                    u1 = P.emit("vector", lambda v, o=g_mean[:, 0:1], i=g_sum[:, 0:1]:
                                v.tensor_scalar_mul(o, i, INVD), inc="ve")
                    P.wait("vector", "ve", u1)
                    u2 = P.emit("vector", lambda v, o=g_var[:, 0:1], m=g_mean[:, 0:1]:
                                v.tensor_tensor(o, m, m, ALU.mult), inc="ve")
                    P.wait("vector", "ve", u2)
                    u3 = P.emit("vector", lambda v, o=g_var[:, 0:1], i=g_ssq[:, 0:1],
                                m=g_var[:, 0:1]:
                                v.tensor_scalar(o, i, INVD, m, ALU.mult, ALU.subtract), inc="ve")
                    P.wait("scalar", "ve", u3)
                    u4 = P.emit("scalar", lambda sc, o=g_std[:, 0:1], i=g_var[:, 0:1]:
                                sc.activation(o, i, AF.Sqrt, bias=LNEPS), inc="se")
                    P.wait("vector", "se", u4)
                    u5 = P.emit("vector", lambda v, o=g_rstd[:, 0:1], i=g_std[:, 0:1]:
                                v.reciprocal(o, i), inc="ve")
                    P.wait("vector", "ve", u5)
                    P.claim("vector", ("lnn", part))
                    u6 = P.emit("vector", lambda v, o=lnn[:, part, :], i=ffo[:, tl, :],
                                m=g_mean[:, 0:1], r=g_rstd[:, 0:1]:
                                v.tensor_scalar(o, i, m, r, ALU.subtract, ALU.mult), inc="ve")
                    P.use(("stat", 0), "ve", u6)
                    P.use(("ffo", tl), "ve", u6)
                    P.wait("gpsimd", *spillw[t])
                    P.wait("gpsimd", "ve", u6)
                    acsn, acv = P.dma("gpsimd",
                                      lambda gg, d=lnn[:, part, :], s=xp_s[t]:
                                      gg.dma_start(d, s, accum_op=ALU.add), f"ac{part}")
                    P.wait("gpsimd", acsn, acv)
                    yosn, ov = P.dma("gpsimd",
                                     lambda gg, d=y_out[b, t * 128:(t + 1) * 128, :], s=lnn[:, part, :]:
                                     gg.dma_start(d, s), f"yo{part}")
                    P.use(("lnn", part), yosn, ov)
                    out_dma_vals.append((yosn, ov))

        P.fence()
    except _StopBuild:
        pass

    for sn, eng in list(P.dmastreams.items()):
        P.wait(eng, sn, P.cur(sn))
    P.finalize()
    return nc


_CACHE = {}


def _get_nc():
    if "nc" not in _CACHE:
        _CACHE["nc"] = build_program()
    return _CACHE["nc"]


def _prep_weights(wq, wk, wv, wo, w_gate, w_up, w_down):
    def attn_tile(w):
        return np.ascontiguousarray(
            np.asarray(w).reshape(NHC, 128, NHC, 128).transpose(0, 3, 2, 1))

    wq_t = attn_tile(wq)
    wk_t = attn_tile(wk)
    wv_t = attn_tile(wv)
    wo_t = np.ascontiguousarray(np.asarray(wo).T.reshape(NHC, 128, D))
    wg_p = np.asarray(w_gate).reshape(D, DI // 4, 4).transpose(0, 2, 1).reshape(D, DI)
    wu_p = np.asarray(w_up).reshape(D, DI // 4, 4).transpose(0, 2, 1).reshape(D, DI)
    wd_p = np.asarray(w_down).reshape(DI // 4, 4, D).transpose(1, 0, 2).reshape(DI, D)
    wg_t = np.ascontiguousarray(wg_p.reshape(D, NIC, 128).transpose(1, 0, 2)
                                .reshape(NIC, NHC, 128, 128).transpose(0, 2, 1, 3))
    wu_t = np.ascontiguousarray(wu_p.reshape(D, NIC, 128).transpose(1, 0, 2)
                                .reshape(NIC, NHC, 128, 128).transpose(0, 2, 1, 3))
    wd_t = np.ascontiguousarray(wd_p.reshape(NIC, 128, D))
    return dict(wq_t=wq_t, wk_t=wk_t, wv_t=wv_t, wo_t=wo_t,
                wg_t=wg_t, wu_t=wu_t, wd_t=wd_t)


def kernel(L0_attn_hidden_states, L0_attn_wq, L0_attn_wk, L0_attn_wv, L0_attn_wo,
           L0_ffn_w_gate, L0_ffn_w_up, L0_ffn_w_down):
    x = np.asarray(L0_attn_hidden_states)
    wdict = _prep_weights(L0_attn_wq, L0_attn_wk, L0_attn_wv, L0_attn_wo,
                          L0_ffn_w_gate, L0_ffn_w_up, L0_ffn_w_down)
    bf = ml_dtypes.bfloat16
    consts = dict(
        c_id=np.eye(128, dtype=bf),
        c_mask=np.ascontiguousarray(np.tril(np.ones((128, 128), np.float32)).T.astype(bf)),
        c_onesc=np.ones((128, 1), bf),
        c_onesr=np.ones((1, 128), bf),
    )
    nc = _get_nc()
    in_maps = []
    for c in range(NCORES):
        m = dict(x=np.ascontiguousarray(x[c * NB:(c + 1) * NB]))
        m.update(wdict)
        m.update(consts)
        in_maps.append(m)
    res = run_bass_kernel_spmd(nc, in_maps, core_ids=list(range(NCORES)))
    out = np.concatenate([r["y"] for r in res.results], axis=0)
    return out


# version 12
# speedup vs baseline: 1.1747x; 1.1342x over previous
import math
import os
import numpy as np
import ml_dtypes
from contextlib import ExitStack

import concourse.bass as bass
import concourse.mybir as mybir
from concourse.bass_utils import run_bass_kernel_spmd

F32 = mybir.dt.float32
BF16 = mybir.dt.bfloat16
AF = mybir.ActivationFunctionType
ALU = mybir.AluOpType
AX = mybir.AxisListType

NCORES = 8
NB = 2
S = 2048
D = 2048
NT = S // 128
NH = 16
HD = 128
DI = 8192
NIC = DI // 128
NHC = D // 128
SCALE = 1.0 / math.sqrt(HD)
EXPC = 10.0
LNEPS = 1e-5
INVD = 1.0 / D

ENGINES = ["tensor", "vector", "scalar", "gpsimd", "sync"]
KPHASES = int(os.environ.get('KPHASES', '99'))
KF_NOMASK = os.environ.get('KF_NOMASK') == '1'
KF_NOPV = os.environ.get('KF_NOPV') == '1'
KF_NOZ = os.environ.get('KF_NOZ') == '1'
KF_NOEXP = os.environ.get('KF_NOEXP') == '1'
KF_NOST = os.environ.get('KF_NOST') == '1'
KF_QUADS = int(os.environ.get('KF_QUADS', '4'))
KF_HEADS = int(os.environ.get('KF_HEADS', str(NH)))


class _StopBuild(Exception):
    pass


class Prog:

    def __init__(self, nc):
        self.nc = nc
        self.streams = {e: [] for e in ENGINES}
        self.sems = {}
        self.ctx = ExitStack()
        self.bufuse = {}
        self.dmastreams = {}
        self.watermark = {}

    def sem(self, name):
        if name not in self.sems:
            h = self.ctx.enter_context(self.nc.semaphore(name))
            self.sems[name] = [h, 0]
        return self.sems[name]

    def dma(self, engine, fn, stream):
        name = "D_" + stream
        self.dmastreams[name] = engine
        return name, self.emit(engine, fn, inc=name, inc_by=16)

    def emit(self, engine, fn, inc=None, inc_by=1):
        rec = None
        val = None
        if inc is not None:
            s = self.sem(inc)
            s[1] += inc_by
            rec = (s[0], inc_by)
            val = s[1]
        self.streams[engine].append((fn, rec))
        return val

    def wait(self, engine, semname, value):
        if value is None or value <= 0:
            return
        wm = self.watermark.get((engine, semname), 0)
        if value <= wm:
            return
        self.watermark[(engine, semname)] = value
        s = self.sem(semname)
        assert value <= s[1], f"wait {semname}>={value} but only {s[1]} recorded"
        self.streams[engine].append(
            (lambda eng, h=s[0], v=value: eng.wait_ge(h, v), None))

    def cur(self, semname):
        return self.sem(semname)[1]

    def use(self, key, semname, value):
        d = self.bufuse.setdefault(key, {})
        if value is not None and value > d.get(semname, 0):
            d[semname] = value

    def claim(self, engine, key):
        for semname, value in self.bufuse.get(key, {}).items():
            self.wait(engine, semname, value)
        self.bufuse[key] = {}

    def waituses(self, engine, key):
        for semname, value in self.bufuse.get(key, {}).items():
            self.wait(engine, semname, value)

    def fence(self, engines=("tensor", "vector", "scalar", "gpsimd", "sync")):
        name = "fb"
        for sn, eng in list(self.dmastreams.items()):
            self.wait(eng, sn, self.cur(sn))
        for e in engines:
            if e in ("tensor", "vector", "scalar"):
                self.streams[e].append((lambda eng: eng.drain(), None))
            self.emit(e, lambda eng: eng.nop(), inc=name)
        v = self.cur(name)
        for e in engines:
            self.wait(e, name, v)
        self.bufuse = {}

    def finalize(self):
        with self.nc.Block() as block:
            for e in ENGINES:
                lst = self.streams[e]
                if not lst:
                    continue

                def body(eng, lst=lst):
                    for fn, rec in lst:
                        inst = fn(eng)
                        if rec is not None:
                            inst.then_inc(rec[0], rec[1])

                getattr(block, e)(body)
        try:
            self.ctx.close()
        except AssertionError:
            pass


def build_program():
    nc = bass.Bass("TRN2", target_bir_lowering=False)

    x_in = nc.dram_tensor("x", [NB, S, D], F32, kind="ExternalInput").ap()
    wq_f = nc.dram_tensor("wq_t", [NHC, 128, NHC, 128], F32, kind="ExternalInput").ap()
    wk_f = nc.dram_tensor("wk_t", [NHC, 128, NHC, 128], F32, kind="ExternalInput").ap()
    wv_f = nc.dram_tensor("wv_t", [NHC, 128, NHC, 128], F32, kind="ExternalInput").ap()
    wo_f = nc.dram_tensor("wo_t", [NHC, 128, D], F32, kind="ExternalInput").ap()
    wg_f = nc.dram_tensor("wg_t", [NIC, 128, NHC, 128], F32, kind="ExternalInput").ap()
    wu_f = nc.dram_tensor("wu_t", [NIC, 128, NHC, 128], F32, kind="ExternalInput").ap()
    wd_f = nc.dram_tensor("wd_t", [NIC, 128, D], F32, kind="ExternalInput").ap()
    c_id = nc.dram_tensor("c_id", [128, 128], BF16, kind="ExternalInput").ap()
    c_mask = nc.dram_tensor("c_mask", [128, 128], BF16, kind="ExternalInput").ap()
    c_onesc = nc.dram_tensor("c_onesc", [128, 1], BF16, kind="ExternalInput").ap()
    c_onesr = nc.dram_tensor("c_onesr", [1, 128], BF16, kind="ExternalInput").ap()
    y_out = nc.dram_tensor("y", [NB, S, D], F32, kind="ExternalOutput").ap()

    wq_b = nc.dram_tensor("wq_b", [NHC, 128, NHC, 128], BF16, kind="Internal").ap()
    wk_b = nc.dram_tensor("wk_b", [NHC, 128, NHC, 128], BF16, kind="Internal").ap()
    wv_b = nc.dram_tensor("wv_b", [NHC, 128, NHC, 128], BF16, kind="Internal").ap()
    wo_b = nc.dram_tensor("wo_b", [NHC, 128, D], BF16, kind="Internal").ap()
    wg_b = nc.dram_tensor("wg_b", [NIC, 128, NHC, 128], BF16, kind="Internal").ap()
    wu_b = nc.dram_tensor("wu_b", [NIC, 128, NHC, 128], BF16, kind="Internal").ap()
    wd_b = nc.dram_tensor("wd_b", [NIC, 128, D], BF16, kind="Internal").ap()
    qt_s = nc.dram_tensor("qt_s", [NH, 128, S], BF16, kind="Internal").ap()
    kt_s = nc.dram_tensor("kt_s", [NH, 128, S], BF16, kind="Internal").ap()
    ot_s = nc.dram_tensor("ot_s", [4, 128, NH, 512], BF16, kind="Internal").ap()
    xp_s = nc.dram_tensor("xp_s", [NT, 128, D], F32, kind="Internal").ap()

    P = Prog(nc)
    E = P.ctx.enter_context

    g_id = E(nc.sbuf_tensor("g_id", [128, 128], BF16))
    g_mask = E(nc.sbuf_tensor("g_mask", [128, 128], BF16))
    g_onesc = E(nc.sbuf_tensor("g_onesc", [128, 1], BF16))
    g_onesr = E(nc.sbuf_tensor("g_onesr", [1, 128], BF16))
    g_onesrf = E(nc.sbuf_tensor("g_onesrf", [1, 128], F32))
    g_cpool = E(nc.sbuf_tensor("g_cpool", [128, 2], F32))
    NSTAT = 4
    g_sum = E(nc.sbuf_tensor("g_sum", [128, NSTAT], F32))
    g_ssq = E(nc.sbuf_tensor("g_ssq", [128, NSTAT], F32))
    g_mean = E(nc.sbuf_tensor("g_mean", [128, NSTAT], F32))
    g_var = E(nc.sbuf_tensor("g_var", [128, NSTAT], F32))
    g_std = E(nc.sbuf_tensor("g_std", [128, NSTAT], F32))
    g_rstd = E(nc.sbuf_tensor("g_rstd", [128, NSTAT], F32))

    consts = [(-EXPC, 0), (LNEPS, 1)]
    for cv, j in consts:
        nc.const_aps.register(F32, cv, g_cpool[:, j:j + 1])

    for cv, j in consts:
        P.emit("gpsimd", lambda g, a=g_cpool[:, j:j + 1], v=cv: g.memset(a, v), inc="init")
    P.emit("gpsimd", lambda g, a=g_onesrf[:]: g.memset(a, 1.0), inc="init")
    for dst, csrc in ((g_id, c_id), (g_mask, c_mask), (g_onesc, c_onesc), (g_onesr, c_onesr)):
        P.dma("gpsimd", lambda g, d=dst[:], s=csrc: g.dma_start(d, s), "cst")
    P.emit("gpsimd", lambda g: g.nop(), inc="init")
    v_init = P.cur("init")
    gv_init = P.cur("D_cst")
    for e in ("tensor", "vector", "scalar", "sync"):
        P.wait(e, "init", v_init)
        P.wait(e, "D_cst", gv_init)

    wdone = {}
    _castq = []

    def _mk_pieces():
        for name, wsrc, dst, parts in (("wq", wq_f, wq_b, 4), ("wk", wk_f, wk_b, 4),
                                       ("wv", wv_f, wv_b, 4), ("wv2", None, None, 0),
                                       ("wo", wo_f, wo_b, 4),
                                       ("wg", wg_f, wg_b, 16), ("wu", wu_f, wu_b, 16),
                                       ("wd", wd_f, wd_b, 16)):
            if parts == 0:
                continue
            n0 = wsrc.shape[0]
            step = n0 // parts
            for i in range(parts):
                _castq.append((name, wsrc[i * step:(i + 1) * step],
                               dst[i * step:(i + 1) * step], i == parts - 1))

    _mk_pieces()

    def pace_cast(n=1):
        for _ in range(n):
            if not _castq:
                return
            name, wsrc, dst, last = _castq.pop(0)
            P.wait("gpsimd", "pe", P.cur("pe"))
            sn, _ = P.dma("gpsimd",
                          lambda g, d=dst, s=wsrc: g.dma_start(d, s), "wr" + name)
            if last:
                wdone[name] = (sn, P.cur(sn))

    def emit_ln(x_ap, out_ap, spar, dump, dumpkey, ready=()):
        for sn, v in ready:
            P.wait("scalar", sn, v)
        P.claim("scalar", dumpkey)
        P.claim("scalar", ("stat", spar))
        v_id = P.emit("scalar", lambda sc, o=dump, i=x_ap, a=g_sum[:, spar:spar + 1]:
                      sc.activation(o, i, AF.Identity, accum_out=a), inc="se")
        P.wait("scalar", "se", v_id)
        v_se = P.emit("scalar", lambda sc, o=dump, i=x_ap, a=g_ssq[:, spar:spar + 1]:
                      sc.activation(o, i, AF.Square, accum_out=a), inc="se")
        P.use(dumpkey, "se", v_se)
        P.wait("vector", "se", v_se)
        v1 = P.emit("vector", lambda v, o=g_mean[:, spar:spar + 1], i=g_sum[:, spar:spar + 1]:
                    v.tensor_scalar_mul(o, i, INVD), inc="ve")
        P.wait("vector", "ve", v1)
        v2 = P.emit("vector", lambda v, o=g_var[:, spar:spar + 1], m=g_mean[:, spar:spar + 1]:
                    v.tensor_tensor(o, m, m, ALU.mult), inc="ve")
        P.wait("vector", "ve", v2)
        v3 = P.emit("vector", lambda v, o=g_var[:, spar:spar + 1], i=g_ssq[:, spar:spar + 1],
                    m=g_var[:, spar:spar + 1]:
                    v.tensor_scalar(o, i, INVD, m, ALU.mult, ALU.subtract), inc="ve")
        P.wait("scalar", "ve", v3)
        v4 = P.emit("scalar", lambda sc, o=g_std[:, spar:spar + 1], i=g_var[:, spar:spar + 1]:
                    sc.activation(o, i, AF.Sqrt, bias=LNEPS), inc="se")
        P.wait("vector", "se", v4)
        v5 = P.emit("vector", lambda v, o=g_rstd[:, spar:spar + 1], i=g_std[:, spar:spar + 1]:
                    v.reciprocal(o, i), inc="ve")
        P.wait("vector", "ve", v5)
        v6 = P.emit("vector", lambda v, o=out_ap, i=x_ap, m=g_mean[:, spar:spar + 1],
                    r=g_rstd[:, spar:spar + 1]:
                    v.tensor_scalar(o, i, m, r, ALU.subtract, ALU.mult), inc="ve")
        P.use(("stat", spar), "ve", v6)
        return v6

    out_dma_vals = []

    NBRUN = min(NB, int(os.environ.get('KNB', str(NB))))
    try:
      for b in range(NBRUN):
        with ExitStack() as sAV:
            EV = sAV.enter_context
            V = EV(nc.sbuf_tensor(f"V{b}", [128, NT, D], BF16))

            with ExitStack() as sA:
                EA = sA.enter_context
                lnT = EA(nc.sbuf_tensor(f"lnT{b}", [128, NHC, S], BF16))
                wstat = EA(nc.sbuf_tensor(f"wstat{b}", [128, 3, NHC, 128], BF16))
                xb = EA(nc.sbuf_tensor(f"xbA{b}", [128, 4, D], F32))
                lnb = EA(nc.sbuf_tensor(f"lnbA{b}", [128, 2, D], BF16))
                eq = EA(nc.sbuf_tensor(f"eqA{b}", [128, 2, D], BF16))
                dumpA = EA(nc.sbuf_tensor(f"dumpA{b}", [128, D], F32))
                PH = [EA(nc.psum_tensor(f"PH{b}{i}", [128, 1024], F32)) for i in range(2)]
                g_pt = [EA(nc.psum_tensor(f"ptA{b}{i}", [128, 1024], BF16)) for i in range(2)]

                lnT_done = 0
                INVD2 = INVD * INVD
                ldv = {}
                v_se = {}
                v_var = {}
                v_std = {}
                lv = {}

                def a1_ld(t):
                    sl = t % 4
                    P.claim("scalar", ("xbA", sl))
                    ldv[t] = P.dma("scalar",
                                   lambda g, d=xb[:, sl, :], s=x_in[b, t * 128:(t + 1) * 128, :]:
                                   g.dma_start(d, s), f"xA{sl}")

                def a1_s1(t):
                    sl = t % 4
                    P.wait("scalar", *ldv[t])
                    P.claim("scalar", ("stat", sl))
                    P.claim("scalar", "dumpA")
                    vid = P.emit("scalar", lambda sc, o=dumpA[:], i=xb[:, sl, :],
                                 a=g_sum[:, sl:sl + 1]:
                                 sc.activation(o, i, AF.Identity, accum_out=a), inc="se")
                    P.wait("scalar", "se", vid)
                    v_se[t] = P.emit("scalar", lambda sc, o=dumpA[:], i=xb[:, sl, :],
                                     a=g_ssq[:, sl:sl + 1]:
                                     sc.activation(o, i, AF.Square, accum_out=a), inc="se")
                    P.use("dumpA", "se", v_se[t])

                def a1_s2(t):
                    sl = t % 4
                    P.wait("vector", "se", v_se[t])
                    m1 = P.emit("vector", lambda v, o=g_var[:, sl:sl + 1], i=g_sum[:, sl:sl + 1],
                                s1=g_sum[:, sl:sl + 1]:
                                v.tensor_scalar(o, i, s1, INVD2, ALU.mult, ALU.mult), inc="ve")
                    P.emit("vector", lambda v, o=g_mean[:, sl:sl + 1], i=g_sum[:, sl:sl + 1]:
                           v.tensor_scalar_mul(o, i, INVD), inc="ve")
                    P.wait("vector", "ve", m1)
                    v_var[t] = P.emit("vector", lambda v, o=g_var[:, sl:sl + 1],
                                      i=g_ssq[:, sl:sl + 1], m=g_var[:, sl:sl + 1]:
                                      v.tensor_scalar(o, i, INVD, m, ALU.mult, ALU.subtract),
                                      inc="ve")

                def a1_s3(t):
                    sl = t % 4
                    P.wait("scalar", "ve", v_var[t])
                    v_std[t] = P.emit("scalar", lambda sc, o=g_std[:, sl:sl + 1],
                                      i=g_var[:, sl:sl + 1]:
                                      sc.activation(o, i, AF.Sqrt, bias=LNEPS), inc="se")

                def a1_s4(t):
                    sl = t % 4
                    P.wait("vector", "se", v_std[t])
                    r1 = P.emit("vector", lambda v, o=g_rstd[:, sl:sl + 1], i=g_std[:, sl:sl + 1]:
                                v.reciprocal(o, i), inc="ve")
                    P.wait("vector", "ve", r1)
                    P.claim("vector", ("lnbA", t % 2))
                    lv[t] = P.emit("vector", lambda v, o=lnb[:, t % 2, :], i=xb[:, sl, :],
                                   m=g_mean[:, sl:sl + 1], r=g_rstd[:, sl:sl + 1]:
                                   v.tensor_scalar(o, i, m, r, ALU.subtract, ALU.mult), inc="ve")
                    P.use(("xbA", sl), "ve", lv[t])
                    P.use(("stat", sl), "ve", lv[t])

                def a1_tr(t):
                    nonlocal lnT_done
                    P.wait("tensor", "ve", lv[t])
                    for grp in range(2):
                        P.claim("tensor", ("pt", grp))
                        tv = None
                        for j in range(8):
                            hc = grp * 8 + j
                            tv = P.emit(
                                "tensor",
                                lambda te, o=g_pt[grp][:, j * 128:(j + 1) * 128],
                                i=lnb[:, t % 2, hc * 128:(hc + 1) * 128]:
                                te.transpose(o, i, g_id[:]),
                                inc="pe" if j == 7 else None)
                        P.use(("lnbA", t % 2), "pe", tv)
                        P.wait("vector", "pe", tv)
                        ev = P.emit(
                            "vector",
                            lambda v, o=lnT[:, grp * 8:(grp + 1) * 8, t * 128:(t + 1) * 128],
                            i=g_pt[grp].ap().rearrange("p (c f) -> p c f", c=8):
                            v.tensor_copy(o, i), inc="ve")
                        P.use(("pt", grp), "ve", ev)
                        lnT_done = ev

                for t in range(4):
                    a1_ld(t)
                for step in range(NT + 3):
                    if b == 0 and step % 4 == 3:
                        pace_cast(1)
                    if step < NT:
                        a1_s1(step)
                    if 0 <= step - 1 < NT:
                        a1_s2(step - 1)
                    if 0 <= step - 2 < NT:
                        a1_s3(step - 2)
                    if 0 <= step - 3 < NT:
                        a1_s4(step - 3)
                        if step + 1 < NT:
                            a1_ld(step + 1)
                        a1_tr(step - 3)

                def qkv_phase(w_bf, spill_dst, wname, is_v):
                    P.wait("sync", *wdone[wname])
                    spills = {}
                    loadv = {}

                    def load(oc):
                        wpar = oc % 3
                        P.claim("sync", ("wstat", wpar))
                        loadv[oc] = P.dma(
                            "sync", lambda s, d=wstat[:, wpar], src=w_bf[oc]:
                            s.dma_start(d, src), f"ws{wpar}")

                    load(0)
                    load(1)
                    load(2)
                    for oc in range(NHC):
                        par = oc % 2
                        wpar = oc % 3
                        if b == 0 and oc % 4 == 1:
                            pace_cast(1)
                        P.wait("tensor", *loadv[oc])
                        P.wait("tensor", "ve", lnT_done)
                        mmv = [0, 0]
                        for sh in range(2):
                            P.claim("tensor", ("PH", sh))
                            for hc in range(NHC):
                                for half in range(2):
                                    st0 = (hc == 0)
                                    sp0 = (hc == NHC - 1)
                                    vv = P.emit(
                                        "tensor",
                                        lambda te, o=PH[sh][:, half * 512:(half + 1) * 512],
                                        w=wstat[:, wpar, hc, :],
                                        r=lnT[:, hc, sh * 1024 + half * 512:sh * 1024 + (half + 1) * 512],
                                        s0=st0, s1=sp0:
                                        te.matmul(o, w, r, start=s0, stop=s1),
                                        inc="pe" if (sp0 and half == 1) else None)
                            mmv[sh] = vv
                            P.use(("wstat", wpar), "pe", vv)
                            P.wait("vector", "pe", vv)
                            P.claim("vector", ("eq", par, sh))
                            ev = P.emit(
                                "vector",
                                lambda v, o=eq[:, par, sh * 1024:(sh + 1) * 1024], i=PH[sh][:]:
                                v.tensor_copy(o, i), inc="ve")
                            P.use(("PH", sh), "ve", ev)
                            if is_v:
                                P.wait("tensor", "ve", ev)
                                P.claim("tensor", ("pt", sh))
                                tv = None
                                for j in range(8):
                                    tv = P.emit(
                                        "tensor",
                                        lambda te, o=g_pt[sh][:, j * 128:(j + 1) * 128],
                                        i=eq[:, par, sh * 1024 + j * 128:sh * 1024 + (j + 1) * 128]:
                                        te.transpose(o, i, g_id[:]),
                                        inc="pe" if j == 7 else None)
                                P.use(("eq", par, sh), "pe", tv)
                                P.wait("vector", "pe", tv)
                                vev = P.emit(
                                    "vector",
                                    lambda v, o=V[:, sh * 8:(sh + 1) * 8, oc * 128:(oc + 1) * 128],
                                    i=g_pt[sh].ap().rearrange("p (c f) -> p c f", c=8):
                                    v.tensor_copy(o, i), inc="ve")
                                P.use(("pt", sh), "ve", vev)
                        if not is_v:
                            P.wait("sync", "ve", P.cur("ve"))
                            ssn, sv = P.dma(
                                "sync", lambda g, d=spill_dst[oc], s=eq[:, par, :]:
                                g.dma_start(d, s), f"sp{par}")
                            P.use(("eq", par, 0), ssn, sv)
                            P.use(("eq", par, 1), ssn, sv)
                            spills[ssn] = sv
                        if oc + 3 < NHC:
                            load(oc + 3)
                    return spills

                if KPHASES <= 1:
                    raise _StopBuild
                q_spill_done = qkv_phase(wq_b, qt_s, "wq", False)
                k_spill_done = qkv_phase(wk_b, kt_s, "wk", False)
                qkv_phase(wv_b, None, "wv", True)
                if KPHASES <= 2:
                    raise _StopBuild

            P.fence()

            with ExitStack() as sF:
                EF = sF.enter_context
                kT = EF(nc.sbuf_tensor(f"kT{b}", [128, NH, S], BF16))
                qTq = EF(nc.sbuf_tensor(f"qTq{b}", [128, NH, 512], BF16))
                ET = [EF(nc.sbuf_tensor(f"ET{b}{i}", [128, NT, 512], BF16)) for i in range(2)]
                otq = EF(nc.sbuf_tensor(f"otq{b}", [128, NH, 512], BF16))
                crow = EF(nc.sbuf_tensor(f"crow{b}", [1, 2, 512], BF16))
                bcs = EF(nc.sbuf_tensor(f"bcs{b}", [128, 512], F32))
                ST = [EF(nc.psum_tensor(f"ST{b}{i}", [128, 1024], F32)) for i in range(2)]
                OTp = [EF(nc.psum_tensor(f"OTp{b}{i}", [128, 512], F32)) for i in range(2)]
                Zp = EF(nc.psum_tensor(f"Zp{b}", [1, 512], F32))
                BCp = EF(nc.psum_tensor(f"BCp{b}", [128, 512], F32))

                ot_spill = {}
                for p in range(KF_QUADS):
                    qc0 = p * 512
                    for sn, vv in list(q_spill_done.items()) + list(k_spill_done.items()):
                        P.wait("sync", sn, vv)
                    P.claim("sync", "qTq")
                    for hh in range(NH):
                        qsn, qt_load = P.dma(
                            "sync", lambda s, d=qTq[:, hh, :], src=qt_s[hh, :, qc0:qc0 + 512]:
                            s.dma_start(d, src), "fq")
                    for hh in range(NH):
                        ksn, kt_load = P.dma(
                            "sync", lambda s, d=kT[:, hh, qc0:qc0 + 512],
                            src=kt_s[hh, :, qc0:qc0 + 512]:
                            s.dma_start(d, src), "fk")

                    kmax = 4 * p + 3
                    stv_all = {}
                    expv_all = {}
                    mask_all = {}
                    pvz_all = {}
                    rv_all = {}
                    bcv_all = {}
                    oev_all = {}

                    def st_stage(h):
                        if b == 0:
                            pace_cast(1)
                        etb = ET[h % 2]
                        P.wait("tensor", "D_fq", qt_load)
                        P.wait("tensor", "D_fk", kt_load)
                        stv = {}
                        expv = {}
                        P.claim("scalar", ("ET", h % 2))
                        for kc0 in range(0, kmax + 1, 2):
                            stp = (kc0 // 2) % 2
                            P.claim("tensor", ("ST", stp))
                            offs = []
                            for kc in (kc0, kc0 + 1):
                                if kc > kmax:
                                    continue
                                off = max(kc - 4 * p, 0) * 128
                                offs.append((kc, off))
                                stv[kc] = P.emit(
                                    "tensor",
                                    lambda te, o=ST[stp][:, (kc % 2) * 512 + off:(kc % 2) * 512 + 512],
                                    w=kT[:, h, kc * 128:(kc + 1) * 128], r=qTq[:, h, off:512]:
                                    te.matmul(o, w, r, start=True, stop=True), inc="pe")
                                P.use("qTq", "pe", stv[kc])
                            P.wait("scalar", "pe", stv[offs[-1][0]])
                            if len(offs) == 2 and offs[0][1] == 0 and offs[1][1] == 0:
                                ev = P.emit(
                                    "scalar",
                                    lambda sc, o=etb[:, kc0:kc0 + 2, :], i=ST[stp][:]:
                                    sc.activation(o, i, AF.Exp, bias=-EXPC, scale=SCALE),
                                    inc="se")
                                expv[kc0] = expv[kc0 + 1] = ev
                            else:
                                for kc, off in offs:
                                    expv[kc] = P.emit(
                                        "scalar",
                                        lambda sc, o=etb[:, kc, off:512],
                                        i=ST[stp][:, (kc % 2) * 512 + off:(kc % 2) * 512 + 512]:
                                        sc.activation(o, i, AF.Exp, bias=-EXPC, scale=SCALE),
                                        inc="se")
                            P.use(("ST", stp), "se", expv[offs[-1][0]])
                        mv = None
                        for j in range(4):
                            kc = 4 * p + j
                            off = j * 128
                            P.wait("vector", "se", expv[kc])
                            mv = P.emit(
                                "vector",
                                lambda v, a=etb[:, kc, off:off + 128]:
                                v.tensor_tensor(a, a, g_mask[:], ALU.mult), inc="ve")
                        stv_all[h] = stv
                        expv_all[h] = expv
                        mask_all[h] = mv

                    def pvz_stage(h):
                        etb = ET[h % 2]
                        P.wait("tensor", "se", expv_all[h][kmax])
                        P.wait("tensor", "ve", mask_all[h])
                        otp = OTp[h % 2]
                        P.claim("tensor", ("OTp", h % 2))
                        P.claim("tensor", "Zp")
                        for kc in range(kmax + 1):
                            off = max(kc - 4 * p, 0) * 128
                            P.emit(
                                "tensor",
                                lambda te, o=otp[:, off:512], w=V[:, kc, h * 128:(h + 1) * 128],
                                r=etb[:, kc, off:512], s0=(kc == 0), s1=(kc == kmax):
                                te.matmul(o, w, r, start=s0, stop=s1))
                        pvz = None
                        for kc in range(kmax + 1):
                            off = max(kc - 4 * p, 0) * 128
                            pvz = P.emit(
                                "tensor",
                                lambda te, o=Zp[0:1, off:512], w=g_onesc[:],
                                r=etb[:, kc, off:512], s0=(kc == 0), s1=(kc == kmax):
                                te.matmul(o, w, r, start=s0, stop=s1), inc="pe")
                        P.use(("ET", h % 2), "pe", pvz)
                        pvz_all[h] = pvz
                        P.wait("vector", "pe", pvz)
                        P.claim("vector", ("crow", h % 2))

                        def _recipb(v, o=crow[0:1, h % 2, :], i=Zp[0:1, :], _nc=nc):
                            with _nc.allow_low_precision("crow bf16: per-row scale"):
                                return v.reciprocal(o, i)
                        rv = P.emit("vector", _recipb, inc="ve")
                        P.use("Zp", "ve", rv)
                        rv_all[h] = rv

                    def bc_stage(h):
                        otp = OTp[h % 2]
                        P.wait("tensor", "ve", rv_all[h])
                        P.claim("tensor", "BCp")
                        bcv = P.emit("tensor",
                                     lambda te, o=BCp[:], w=g_onesr[:], r=crow[0:1, h % 2, :]:
                                     te.matmul(o, w, r, start=True, stop=True), inc="pe")
                        P.use(("crow", h % 2), "pe", bcv)
                        bcv_all[h] = bcv
                        P.wait("vector", "pe", bcv)
                        P.claim("vector", "bcs")
                        cv = P.emit("vector", lambda v, o=bcs[:], i=BCp[:]:
                                    v.tensor_copy(o, i), inc="ve")
                        P.use("BCp", "ve", cv)
                        P.claim("vector", ("otq", h))
                        P.wait("vector", "ve", cv)
                        oev = P.emit("vector", lambda v, o=otq[:, h, :], i=otp[:], c=bcs[:]:
                                     v.tensor_tensor(o, i, c, ALU.mult), inc="ve")
                        P.use(("OTp", h % 2), "ve", oev)
                        P.use("bcs", "ve", oev)
                        oev_all[h] = oev

                    for hh in range(NH + 2):
                        if hh < NH:
                            st_stage(hh)
                        if 0 <= hh - 2 < NH:
                            bc_stage(hh - 2)
                        if 0 <= hh - 1 < NH:
                            pvz_stage(hh - 1)
                    P.wait("gpsimd", "ve", P.cur("ve"))
                    osn, sv = P.dma("gpsimd", lambda g, d=ot_s[p], s=otq[:]:
                                    g.dma_start(d, s), "os")
                    for h in range(NH):
                        P.use(("otq", h), osn, sv)
                    ot_spill[p] = (osn, sv)

            P.fence()
            if KPHASES <= 3:
                raise _StopBuild

        with ExitStack() as sW:
            EW = sW.enter_context
            wo_res = EW(nc.sbuf_tensor(f"wo_res{b}", [128, NHC, D], BF16))
            otq_wo = EW(nc.sbuf_tensor(f"otq_wo{b}", [128, 2, NH, 512], BF16))
            xbw = EW(nc.sbuf_tensor(f"xbW{b}", [128, 2, D], F32))
            attn_t = EW(nc.sbuf_tensor(f"attn_t{b}", [128, D], F32))
            lnw = EW(nc.sbuf_tensor(f"lnw{b}", [128, D], F32))
            xpo = EW(nc.sbuf_tensor(f"xpoW{b}", [128, 2, D], F32))
            dumpW = EW(nc.sbuf_tensor(f"dumpW{b}", [128, D], F32))
            PW = [EW(nc.psum_tensor(f"PW{b}{i}", [128, 1024], F32)) for i in range(2)]

            P.wait("sync", *wdone["wo"])
            for fc in range(NHC):
                wsn, wo_load = P.dma("sync", lambda s, d=wo_res[:, fc, :], src=wo_b[fc]:
                                     s.dma_start(d, src), "wo")
            spillw = [0] * NT
            for t in range(NT):
                p = t // 4
                parq = p % 2
                if b == 0:
                    pace_cast(1)
                if t % 4 == 0:
                    P.wait("sync", *ot_spill[p])
                    P.claim("sync", ("otq_wo", parq))
                    oqsn, oq_load = P.dma(
                        "sync", lambda s, d=otq_wo[:, parq], src=ot_s[p]:
                        s.dma_start(d, src), f"oq{parq}")
                P.claim("scalar", ("xbW", t % 2))
                xwsn, xv = P.dma(
                    "scalar", lambda g, d=xbw[:, t % 2, :], s=x_in[b, t * 128:(t + 1) * 128, :]:
                    g.dma_start(d, s), f"xW{t % 2}")
                P.wait("tensor", "D_wo", wo_load)
                P.wait("tensor", oqsn, oq_load)
                P.claim("scalar", ("stat", 0))
                P.claim("scalar", ("stat", 1))
                last_oq_use = None
                for oh in range(2):
                    P.claim("tensor", ("PW", oh))
                    vv = None
                    for fc in range(NHC):
                        for half in range(2):
                            st0 = (fc == 0)
                            sp0 = (fc == NHC - 1)
                            vv = P.emit(
                                "tensor",
                                lambda te, o=PW[oh][:, half * 512:(half + 1) * 512],
                                w=otq_wo[:, parq, fc, (t % 4) * 128:(t % 4) * 128 + 128],
                                r=wo_res[:, fc, oh * 1024 + half * 512:oh * 1024 + (half + 1) * 512],
                                s0=st0, s1=sp0:
                                te.matmul(o, w, r, start=s0, stop=s1),
                                inc="pe" if (sp0 and half == 1) else None)
                    last_oq_use = vv
                    P.wait("scalar", "pe", vv)
                    if oh == 0:
                        P.claim("scalar", ("attn_t",))
                    ev = P.emit(
                        "scalar",
                        lambda sc, o=attn_t[:, oh * 1024:(oh + 1) * 1024], i=PW[oh][:],
                        a=g_sum[:, oh:oh + 1]:
                        sc.activation(o, i, AF.Identity, accum_out=a), inc="se")
                    P.use(("PW", oh), "se", ev)
                P.use(("otq_wo", parq), "pe", last_oq_use)
                P.claim("scalar", ("dumpW",))
                P.wait("scalar", "se", P.cur("se"))
                v_sq = P.emit("scalar", lambda sc, o=dumpW[:], i=attn_t[:], a=g_ssq[:, 0:1]:
                              sc.activation(o, i, AF.Square, accum_out=a), inc="se")
                P.use(("dumpW",), "se", v_sq)
                P.wait("vector", "se", v_sq)
                w1 = P.emit("vector", lambda v, o=g_sum[:, 0:1], a=g_sum[:, 0:1], bb=g_sum[:, 1:2]:
                            v.tensor_tensor(o, a, bb, ALU.add), inc="ve")
                P.wait("vector", "ve", w1)
                w2 = P.emit("vector", lambda v, o=g_mean[:, 0:1], i=g_sum[:, 0:1]:
                            v.tensor_scalar_mul(o, i, INVD), inc="ve")
                P.wait("vector", "ve", w2)
                w3 = P.emit("vector", lambda v, o=g_var[:, 0:1], m=g_mean[:, 0:1]:
                            v.tensor_tensor(o, m, m, ALU.mult), inc="ve")
                P.wait("vector", "ve", w3)
                w4 = P.emit("vector", lambda v, o=g_var[:, 0:1], i=g_ssq[:, 0:1], m=g_var[:, 0:1]:
                            v.tensor_scalar(o, i, INVD, m, ALU.mult, ALU.subtract), inc="ve")
                P.wait("scalar", "ve", w4)
                w5 = P.emit("scalar", lambda sc, o=g_std[:, 0:1], i=g_var[:, 0:1]:
                            sc.activation(o, i, AF.Sqrt, bias=LNEPS), inc="se")
                P.wait("vector", "se", w5)
                w6 = P.emit("vector", lambda v, o=g_rstd[:, 0:1], i=g_std[:, 0:1]:
                            v.reciprocal(o, i), inc="ve")
                P.wait("vector", "ve", w6)
                P.claim("vector", ("lnw",))
                w7 = P.emit("vector", lambda v, o=lnw[:], i=attn_t[:], m=g_mean[:, 0:1],
                            r=g_rstd[:, 0:1]:
                            v.tensor_scalar(o, i, m, r, ALU.subtract, ALU.mult), inc="ve")
                P.use(("attn_t",), "ve", w7)
                P.use(("stat", 0), "ve", w7)
                P.use(("stat", 1), "ve", w7)
                P.wait("vector", xwsn, xv)
                P.wait("vector", "ve", w7)
                P.claim("vector", ("xpo", t % 2))
                w8 = P.emit("vector", lambda v, o=xpo[:, t % 2, :], i=lnw[:], xx=xbw[:, t % 2, :]:
                            v.tensor_tensor(o, i, xx, ALU.add), inc="ve")
                P.use(("lnw",), "ve", w8)
                P.use(("xbW", t % 2), "ve", w8)
                P.wait("gpsimd", "ve", w8)
                xssn, sv = P.dma("gpsimd", lambda g, d=xp_s[t], s=xpo[:, t % 2, :]:
                                 g.dma_start(d, s), f"xs{t % 2}")
                P.use(("xpo", t % 2), xssn, sv)
                spillw[t] = (xssn, sv)

        P.fence()
        if KPHASES <= 4:
            raise _StopBuild

        with ExitStack() as sN:
            EN = sN.enter_context
            lnT2 = EN(nc.sbuf_tensor(f"lnT2{b}", [128, 2, NHC, 512], BF16))
            hspT = EN(nc.sbuf_tensor(f"hspT{b}", [128, NIC, 512], BF16))
            hbuf = EN(nc.sbuf_tensor(f"hbuf{b}", [128, 4, 512], BF16))
            habs = EN(nc.sbuf_tensor(f"habs{b}", [128, 4, 512], BF16))
            htmp = EN(nc.sbuf_tensor(f"htmp{b}", [128, 4, 512], BF16))
            sg = EN(nc.sbuf_tensor(f"sg{b}", [128, 2, 512], F32))
            wgu = EN(nc.sbuf_tensor(f"wgu{b}", [128, 2, 2, NHC, 128], BF16))
            wdc = EN(nc.sbuf_tensor(f"wdc{b}", [128, 2, 8, 512], BF16))
            xbn = EN(nc.sbuf_tensor(f"xbN{b}", [128, 1, D], F32))
            lnbn = EN(nc.sbuf_tensor(f"lnbN{b}", [128, 4, D], BF16))
            ffo = EN(nc.sbuf_tensor(f"ffo{b}", [128, 4, D], BF16))
            lnn = EN(nc.sbuf_tensor(f"lnn{b}", [128, 2, D], F32))
            dumpN = EN(nc.sbuf_tensor(f"dumpN{b}", [128, D], BF16))
            PF = [EN(nc.psum_tensor(f"PF{b}{i}", [128, 512], F32)) for i in range(4)]
            g_pt = [EN(nc.psum_tensor(f"ptN{b}{i}", [128, 1024], BF16)) for i in range(2)]

            P.wait("sync", *wdone["wg"])
            P.wait("sync", *wdone["wu"])
            P.wait("sync", *wdone["wd"])

            ln2_done = {}
            f0_lv = {}

            def emit_F0_ln(g):
                for tl in range(4):
                    t = g * 4 + tl
                    par = tl
                    xpar = 0
                    P.wait("scalar", *spillw[t])
                    P.claim("scalar", ("xbN", xpar))
                    xnsn, xv = P.dma("scalar", lambda gg, d=xbn[:, xpar, :], s=xp_s[t]:
                                     gg.dma_start(d, s), f"xN{xpar}")
                    P.claim("vector", ("lnbN", par))
                    lv = emit_ln(xbn[:, xpar, :], lnbn[:, par, :], 2 + (tl % 2), dumpN[:],
                                 "dumpN", ready=[(xnsn, xv)])
                    P.use(("xbN", xpar), "ve", lv)
                    f0_lv[(g, tl)] = lv

            def emit_F0_tr(g):
                gp = g % 2
                P.claim("vector", ("lnT2", gp))
                for tl in range(4):
                    par = tl
                    P.wait("tensor", "ve", f0_lv[(g, tl)])
                    for grp in range(2):
                        P.claim("tensor", ("pt", grp))
                        tv = None
                        for j in range(8):
                            hc = grp * 8 + j
                            tv = P.emit(
                                "tensor",
                                lambda te, o=g_pt[grp][:, j * 128:(j + 1) * 128],
                                i=lnbn[:, par, hc * 128:(hc + 1) * 128]:
                                te.transpose(o, i, g_id[:]),
                                inc="pe" if j == 7 else None)
                        P.use(("lnbN", par), "pe", tv)
                        P.wait("vector", "pe", tv)
                        ev = P.emit(
                            "vector",
                            lambda v, o=lnT2[:, gp, grp * 8:(grp + 1) * 8, tl * 128:(tl + 1) * 128],
                            i=g_pt[grp].ap().rearrange("p (c f) -> p c f", c=8):
                            v.tensor_copy(o, i), inc="ve")
                        P.use(("pt", grp), "ve", ev)
                        ln2_done[g] = ev

            emit_F0_ln(0)
            emit_F0_tr(0)
            for g in range(4):
                gp = g % 2
                ln2_last = ln2_done[g]
                if g + 1 < 4:
                    emit_F0_ln(g + 1)

                seq = [(c4, sub) for c4 in range(16) for sub in range(4)]
                guload = {}

                def gu_load(idx):
                    c4, sub = seq[idx]
                    parb = sub % 2
                    for pj, wsrc in enumerate((wg_b, wu_b)):
                        ic = c4 + 16 * sub
                        P.claim("sync", ("wgu", parb, pj))
                        guload[(c4, sub, pj)] = P.dma(
                            "sync", lambda s, d=wgu[:, parb, pj], src=wsrc[ic]:
                            s.dma_start(d, src), f"wg{parb}{pj}")

                gu_load(0)
                gu_load(1)
                for idx, (c4, sub) in enumerate(seq):
                    ic = c4 + 16 * sub
                    parb = sub % 2
                    for pj in range(2):
                        bank = 2 * parb + pj
                        P.wait("tensor", *guload[(c4, sub, pj)])
                        P.wait("tensor", "ve", ln2_last)
                        P.claim("tensor", ("PF", bank))
                        vv = None
                        for hc in range(NHC):
                            st0 = (hc == 0)
                            sp0 = (hc == NHC - 1)
                            vv = P.emit(
                                "tensor",
                                lambda te, o=PF[bank][:], w=wgu[:, parb, pj, hc, :],
                                r=lnT2[:, gp, hc, :], s0=st0, s1=sp0:
                                te.matmul(o, w, r, start=s0, stop=s1),
                                inc="pe" if sp0 else None)
                        P.use(("wgu", parb, pj), "pe", vv)
                        P.use(("lnT2", gp), "pe", vv)
                        if pj == 0:
                            gate_mm = vv
                        else:
                            up_mm = vv
                    P.wait("scalar", "pe", gate_mm)
                    P.claim("scalar", ("sg", parb))
                    sl = P.emit("scalar", lambda sc, o=sg[:, parb, :], i=PF[2 * parb][:]:
                                sc.activation(o, i, AF.Silu), inc="se")
                    P.use(("PF", 2 * parb), "se", sl)
                    P.wait("vector", "se", sl)
                    P.wait("vector", "pe", up_mm)
                    P.claim("vector", ("hbuf", sub))
                    hv = P.emit("vector", lambda v, o=hbuf[:, sub, :], i=sg[:, parb, :],
                                u=PF[2 * parb + 1][:]:
                                v.tensor_tensor(o, i, u, ALU.mult), inc="ve")
                    P.use(("sg", parb), "ve", hv)
                    P.use(("PF", 2 * parb + 1), "ve", hv)
                    if sub == 3:
                        def vop(fn, inc=None):
                            return P.emit("vector", fn, inc=inc)
                        P.wait("vector", "ve", hv)
                        for s4 in range(4):
                            vop(lambda v, o=habs[:, s4, :], i=hbuf[:, s4, :]:
                                v.tensor_tensor(o, i, i, ALU.mult),
                                inc="ve" if s4 == 3 else None)
                        P.wait("vector", "ve", P.cur("ve"))
                        vop(lambda v, o=htmp[:, 0, :], a=habs[:, 0, :], bb=habs[:, 1, :]:
                            v.tensor_tensor(o, a, bb, ALU.max))
                        vop(lambda v, o=htmp[:, 1, :], a=habs[:, 0, :], bb=habs[:, 1, :]:
                            v.tensor_tensor(o, a, bb, ALU.min))
                        vop(lambda v, o=htmp[:, 2, :], a=habs[:, 2, :], bb=habs[:, 3, :]:
                            v.tensor_tensor(o, a, bb, ALU.max))
                        vop(lambda v, o=htmp[:, 3, :], a=habs[:, 2, :], bb=habs[:, 3, :]:
                            v.tensor_tensor(o, a, bb, ALU.min), inc="ve")
                        P.wait("vector", "ve", P.cur("ve"))
                        vop(lambda v, o=htmp[:, 0, :], a=htmp[:, 0, :], bb=htmp[:, 2, :]:
                            v.tensor_tensor(o, a, bb, ALU.min))
                        vop(lambda v, o=htmp[:, 1, :], a=htmp[:, 1, :], bb=htmp[:, 3, :]:
                            v.tensor_tensor(o, a, bb, ALU.max), inc="ve")
                        P.wait("vector", "ve", P.cur("ve"))
                        vop(lambda v, o=htmp[:, 0, :], a=htmp[:, 0, :], bb=htmp[:, 1, :]:
                            v.tensor_tensor(o, a, bb, ALU.max), inc="ve")
                        P.wait("vector", "ve", P.cur("ve"))
                        for s4 in range(4):
                            vop(lambda v, o=habs[:, s4, :], a=habs[:, s4, :], bb=htmp[:, 0, :]:
                                v.tensor_tensor(o, a, bb, ALU.is_ge),
                                inc="ve" if s4 == 3 else None)
                        P.wait("vector", "ve", P.cur("ve"))
                        for s4 in range(4):
                            ic4 = c4 + 16 * s4
                            P.claim("vector", ("hspT", ic4))
                            sp = vop(lambda v, o=hspT[:, ic4, :], a=hbuf[:, s4, :],
                                     bb=habs[:, s4, :]:
                                     v.tensor_tensor(o, a, bb, ALU.mult), inc="ve")
                            P.use(("hbuf", s4), "ve", sp)
                            if s4 == 3:
                                for s5 in range(4):
                                    P.use(("spars", c4, s5), "ve", sp)
                    if idx + 2 < len(seq):
                        gu_load(idx + 2)

                if g + 1 < 4:
                    emit_F0_tr(g + 1)

                wdl = {}
                NB8 = NIC // 8

                def wd_block_load(oq, ib):
                    parw = ib % 2
                    P.claim("sync", ("wdc", parw))
                    wdl[(oq, ib)] = P.dma(
                        "sync", lambda s, d=wdc[:, parw],
                        src=wd_b[ib * 8:(ib + 1) * 8, :, oq * 512:(oq + 1) * 512].rearrange(
                            "c p f -> p c f"):
                        s.dma_start(d, src), f"wd{parw}")

                wd_block_load(0, 0)
                wd_block_load(0, 1)
                dn_mm = {}
                dn_evac = {}
                for oq in range(4):
                    for ib in range(NB8):
                        parw = ib % 2
                        for ic in range(ib * 8, ib * 8 + 8):
                            P.waituses("tensor", ("spars", ic % 16, ic // 16))
                        P.wait("tensor", *wdl[(oq, ib)])
                        vv = None
                        for tt in range(4):
                            if ib == 0:
                                P.claim("tensor", ("PF", tt))
                            for j in range(8):
                                ic = ib * 8 + j
                                st0 = (ic == 0)
                                sp0 = (ic == NIC - 1)
                                vv = P.emit(
                                    "tensor",
                                    lambda te, o=PF[tt][:],
                                    w=hspT[:, ic, tt * 128:(tt + 1) * 128],
                                    r=wdc[:, parw, j], s0=st0, s1=sp0:
                                    te.matmul(o, w, r, start=s0, stop=s1),
                                    inc="pe" if (tt == 3 and j == 7) else None)
                        if ib == NB8 - 1:
                            for tt in range(4):
                                dn_mm[(oq, tt)] = vv
                        P.use(("wdc", parw), "pe", vv)
                        nxt = (oq, ib + 2) if ib + 2 < NB8 else \
                              ((oq + 1, ib + 2 - NB8) if oq + 1 < 4 else None)
                        if nxt is not None:
                            wd_block_load(*nxt)
                    if oq == 3:
                        for icx in range(NIC):
                            P.use(("hspT", icx), "pe", P.cur("pe"))
                    for tt in range(4):
                        P.wait("scalar", "pe", dn_mm[(oq, tt)])
                        if oq == 0:
                            P.claim("scalar", ("ffo", tt))
                        ev = P.emit(
                            "scalar",
                            lambda sc, o=ffo[:, tt, oq * 512:(oq + 1) * 512], i=PF[tt][:]:
                            sc.activation(o, i, AF.Identity), inc="se")
                        P.use(("PF", tt), "se", ev)
                        dn_evac[(oq, tt)] = ev

                for tl in range(4):
                    t = g * 4 + tl
                    part = tl % 2
                    P.wait("scalar", "se", dn_evac[(3, tl)])
                    P.claim("scalar", ("stat", 0))
                    P.claim("scalar", "dumpN")
                    v1 = P.emit("scalar", lambda sc, o=dumpN[:], i=ffo[:, tl, :], a=g_sum[:, 0:1]:
                                sc.activation(o, i, AF.Identity, accum_out=a), inc="se")
                    P.wait("scalar", "se", v1)
                    v2 = P.emit("scalar", lambda sc, o=dumpN[:], i=ffo[:, tl, :], a=g_ssq[:, 0:1]:
                                sc.activation(o, i, AF.Square, accum_out=a), inc="se")
                    P.use("dumpN", "se", v2)
                    P.wait("vector", "se", v2)
                    u1 = P.emit("vector", lambda v, o=g_mean[:, 0:1], i=g_sum[:, 0:1]:
                                v.tensor_scalar_mul(o, i, INVD), inc="ve")
                    P.wait("vector", "ve", u1)
                    u2 = P.emit("vector", lambda v, o=g_var[:, 0:1], m=g_mean[:, 0:1]:
                                v.tensor_tensor(o, m, m, ALU.mult), inc="ve")
                    P.wait("vector", "ve", u2)
                    u3 = P.emit("vector", lambda v, o=g_var[:, 0:1], i=g_ssq[:, 0:1],
                                m=g_var[:, 0:1]:
                                v.tensor_scalar(o, i, INVD, m, ALU.mult, ALU.subtract), inc="ve")
                    P.wait("scalar", "ve", u3)
                    u4 = P.emit("scalar", lambda sc, o=g_std[:, 0:1], i=g_var[:, 0:1]:
                                sc.activation(o, i, AF.Sqrt, bias=LNEPS), inc="se")
                    P.wait("vector", "se", u4)
                    u5 = P.emit("vector", lambda v, o=g_rstd[:, 0:1], i=g_std[:, 0:1]:
                                v.reciprocal(o, i), inc="ve")
                    P.wait("vector", "ve", u5)
                    P.claim("vector", ("lnn", part))
                    u6 = P.emit("vector", lambda v, o=lnn[:, part, :], i=ffo[:, tl, :],
                                m=g_mean[:, 0:1], r=g_rstd[:, 0:1]:
                                v.tensor_scalar(o, i, m, r, ALU.subtract, ALU.mult), inc="ve")
                    P.use(("stat", 0), "ve", u6)
                    P.use(("ffo", tl), "ve", u6)
                    P.wait("gpsimd", *spillw[t])
                    P.wait("gpsimd", "ve", u6)
                    acsn, acv = P.dma("gpsimd",
                                      lambda gg, d=lnn[:, part, :], s=xp_s[t]:
                                      gg.dma_start(d, s, accum_op=ALU.add), f"ac{part}")
                    P.wait("gpsimd", acsn, acv)
                    yosn, ov = P.dma("gpsimd",
                                     lambda gg, d=y_out[b, t * 128:(t + 1) * 128, :], s=lnn[:, part, :]:
                                     gg.dma_start(d, s), f"yo{part}")
                    P.use(("lnn", part), yosn, ov)
                    out_dma_vals.append((yosn, ov))

        P.fence()
    except _StopBuild:
        pass

    for sn, eng in list(P.dmastreams.items()):
        P.wait(eng, sn, P.cur(sn))
    P.finalize()
    return nc


_CACHE = {}


def _get_nc():
    if "nc" not in _CACHE:
        _CACHE["nc"] = build_program()
    return _CACHE["nc"]


def _prep_weights(wq, wk, wv, wo, w_gate, w_up, w_down):
    def attn_tile(w):
        return np.ascontiguousarray(
            np.asarray(w).reshape(NHC, 128, NHC, 128).transpose(0, 3, 2, 1))

    wq_t = attn_tile(wq)
    wk_t = attn_tile(wk)
    wv_t = attn_tile(wv)
    wo_t = np.ascontiguousarray(np.asarray(wo).T.reshape(NHC, 128, D))
    wg_p = np.asarray(w_gate).reshape(D, DI // 4, 4).transpose(0, 2, 1).reshape(D, DI)
    wu_p = np.asarray(w_up).reshape(D, DI // 4, 4).transpose(0, 2, 1).reshape(D, DI)
    wd_p = np.asarray(w_down).reshape(DI // 4, 4, D).transpose(1, 0, 2).reshape(DI, D)
    wg_t = np.ascontiguousarray(wg_p.reshape(D, NIC, 128).transpose(1, 0, 2)
                                .reshape(NIC, NHC, 128, 128).transpose(0, 2, 1, 3))
    wu_t = np.ascontiguousarray(wu_p.reshape(D, NIC, 128).transpose(1, 0, 2)
                                .reshape(NIC, NHC, 128, 128).transpose(0, 2, 1, 3))
    wd_t = np.ascontiguousarray(wd_p.reshape(NIC, 128, D))
    return dict(wq_t=wq_t, wk_t=wk_t, wv_t=wv_t, wo_t=wo_t,
                wg_t=wg_t, wu_t=wu_t, wd_t=wd_t)


def kernel(L0_attn_hidden_states, L0_attn_wq, L0_attn_wk, L0_attn_wv, L0_attn_wo,
           L0_ffn_w_gate, L0_ffn_w_up, L0_ffn_w_down):
    x = np.asarray(L0_attn_hidden_states)
    wdict = _prep_weights(L0_attn_wq, L0_attn_wk, L0_attn_wv, L0_attn_wo,
                          L0_ffn_w_gate, L0_ffn_w_up, L0_ffn_w_down)
    bf = ml_dtypes.bfloat16
    consts = dict(
        c_id=np.eye(128, dtype=bf),
        c_mask=np.ascontiguousarray(np.tril(np.ones((128, 128), np.float32)).T.astype(bf)),
        c_onesc=np.ones((128, 1), bf),
        c_onesr=np.ones((1, 128), bf),
    )
    nc = _get_nc()
    in_maps = []
    for c in range(NCORES):
        m = dict(x=np.ascontiguousarray(x[c * NB:(c + 1) * NB]))
        m.update(wdict)
        m.update(consts)
        in_maps.append(m)
    res = run_bass_kernel_spmd(nc, in_maps, core_ids=list(range(NCORES)))
    out = np.concatenate([r["y"] for r in res.results], axis=0)
    return out
